# version 11 (final)
# baseline (speedup 1.0000x reference)
from contextlib import ExitStack

import numpy as np
import concourse.bass as bass
import concourse.mybir as mybir
from concourse.bass_utils import run_bass_kernel_spmd

F32 = mybir.dt.float32
F16 = mybir.dt.float16
I32 = mybir.dt.int32
AF = mybir.ActivationFunctionType
ALU = mybir.AluOpType

T = 2048
C = 1024
NDC = C // 128
HID = 3072
NHC = HID // 128
S127 = 1.0 / 127.0
EPS_LN = 1e-5
EPS_GN = 0.00064

PHASES = ("attn_a", "ffn1", "attn_b", "ffn2")
DBG_FFN = False


class Sems:

    def __init__(self, nc, es):
        object.__setattr__(self, "nc", nc)
        object.__setattr__(self, "es", es)
        object.__setattr__(self, "m", {})

    def ent(self, name):
        if name not in self.m:
            h = self.es.enter_context(self.nc.semaphore(name))
            self.m[name] = [h, 0]
        return self.m[name]

    def post(self, name, instr, n=1):
        e = self.ent(name)
        instr.then_inc(e[0], n)
        e[1] += n
        return e[1]

    def val(self, name):
        return self.ent(name)[1]

    def wait(self, name, eng, v=None):
        e = self.ent(name)
        eng.wait_ge(e[0], e[1] if v is None else v)


class B:
    pass


def build_nc(detect_races=False):
    nc = bass.Bass("TRN2", target_bir_lowering=False,
                   detect_race_conditions=detect_races)
    es = ExitStack()
    nc._keepalive = es

    d = B()
    s = B()
    p = B()

    d.x0 = nc.dram_tensor("x0", [C, T], F32, kind="ExternalInput")
    d.wva = nc.dram_tensor("wva", [C, C], F32, kind="ExternalInput")
    d.bva = nc.dram_tensor("bva", [C], F32, kind="ExternalInput")
    d.wgi = nc.dram_tensor("wgi", [HID, C], I32, kind="ExternalInput")
    d.wui = nc.dram_tensor("wui", [HID, C], I32, kind="ExternalInput")
    d.wdi = nc.dram_tensor("wdi", [C, HID], I32, kind="ExternalInput")
    for n, shp in [("maa_x", [C]), ("maa_w", [C]), ("maa_k", [C]),
                   ("maa_v", [C]), ("maa_r", [C]), ("maa_g", [C]),
                   ("maa_w1", [C, 160]), ("maa_w2", [5, 32, C]),
                   ("Wr", [C, C]), ("Wk", [C, C]), ("Wv", [C, C]),
                   ("Wo", [C, C]), ("gate_w1", [C, 64]), ("gate_w2", [64, C]),
                   ("time_decay", [C]), ("decay_w1", [C, 64]),
                   ("decay_w2", [64, C]), ("u", [16, 64]),
                   ("ln_g", [C]), ("ln_b", [C])]:
        setattr(d, n, nc.dram_tensor(n, shp, F32, kind="ExternalInput"))
    d.out = nc.dram_tensor("out", [C, T], F32, kind="ExternalOutput")
    DBG_P3 = False
    if DBG_P3:
        d.dbg3 = nc.dram_tensor("dbg3", [128, 7 * 2048 + 512 + 64 + 32 + 1024 + 4096 + 2048 + 3 * 128 + 1 + 8 + 2048 + 48],
                                F32, kind="ExternalOutput")
    if DBG_FFN:
        d.dbgxn = nc.dram_tensor("dbgxn", [128, 2 * NDC * 512], F16, kind="ExternalOutput")
        d.dbghid = nc.dram_tensor("dbghid", [128, 2 * NHC * 512], F16, kind="ExternalOutput")
    d.wgT = nc.dram_tensor("wgT_scr", [128, NHC * NDC * 128], F16)
    d.wuT = nc.dram_tensor("wuT_scr", [128, NHC * NDC * 128], F16)
    d.wdT = nc.dram_tensor("wdT_scr", [128, NDC * NHC * 128], F16)

    def make_alloc(stack):
        def sbuf(name, shape, dt=F32):
            t = stack.enter_context(nc.sbuf_tensor(name, shape, dt))
            setattr(s, name, t)
            return t

        def psum(name, shape, dt=F32):
            t = stack.enter_context(nc.psum_tensor(name, shape, dt))
            setattr(p, name, t)
            return t
        return sbuf, psum

    sbuf, psum = make_alloc(es)

    GT = 48
    sbuf("xres", [128, NDC * T])
    sbuf("ones128", [128, 128])
    sbuf("rowv", [128, 1])
    sbuf("bva_pp", [128, NDC])
    sbuf("cst", [128, 4])

    sm = Sems(nc, es)

    HCOLS = 6 * C
    NGR = 4
    STW = [("wgi", NHC, C, 6), ("wui", NHC, C, 6), ("wdi", NDC, HID, 2)]
    WCUM = [0, NHC, 2 * NHC]
    if "attn_a" in PHASES:
        p1_es = ExitStack()
        sbuf1, psum1 = make_alloc(p1_es)
        sbuf1("wva_sb", [128, NDC * C])
        sbuf1("xsq", [128, NDC * 512])
        sbuf1("meanB", [128, 512])
        sbuf1("rstdB", [128, 512])
        sbuf1("varB", [128, 512])
        sbuf1("xn", [128, 2 * NDC * 512])
        sbuf1("nat", [128, HCOLS], F16)
        sbuf1("tpb", [128, HCOLS], F16)
        sbuf1("istage", [128, 2 * HID], I32)
        psum1("pS1", [128, 512])
        psum1("pS2", [128, 512])
        psum1("pA0", [128, 512])
        psum1("pA1", [128, 512])
        with nc.Block("p1") as blk:
            NT = 4

            @blk.gpsimd
            def _(gp):
                gp.iota(s.rowv[:], [[0, 1]], channel_multiplier=1,
                        allow_small_or_imprecise_dtypes=True)
                gp.memset(s.ones128[:], 1.0)
                gp.memset(s.cst[:, 0:1], EPS_LN)
                sm.post("cst_rdy", gp.memset(s.cst[:, 1:2], EPS_GN))
                for dc in range(NDC):
                    sm.post("x0ld", gp.dma_start(
                        out=s.xres[:, dc * T:(dc + 1) * T],
                        in_=d.x0[dc * 128:(dc + 1) * 128, :]), 16)
                sm.post("x0ld", gp.dma_start(
                    out=s.bva_pp[:],
                    in_=bass.AP(d.bva, 0, [[1, 128], [128, NDC]]),
                    allow_slow_non_contiguous=True), 16)
                for dc in range(NDC):
                    sm.post("wvald", gp.dma_start(
                        out=s.wva_sb[:, dc * C:(dc + 1) * C],
                        in_=d.wva[dc * 128:(dc + 1) * 128, :]), 16)
                for wi, dst in enumerate([d.wgT, d.wuT, d.wdT]):
                    for q in range(NGR):
                        sm.wait("sttp", gp, 16 * (NGR * wi + q + 1))
                        sm.post("stout", gp.dma_start(
                            out=dst[:, q * HCOLS:(q + 1) * HCOLS],
                            in_=s.tpb[:]), 16)
                sm.wait("stout", gp)
                sm.wait("x0ld", gp)
                sm.wait("wvald", gp)
                sm.wait("stldE", gp)
                sm.wait("stldO", gp)

            @blk.sync
            def _(sy):
                for wi, (nm, nchunk, row, ch2) in enumerate(STW):
                    src = getattr(d, nm)
                    for ch in range(nchunk):
                        gi = WCUM[wi] + ch
                        if gi > 1:
                            sm.wait("stcast", sy, gi - 1)
                        sm.post("stldE" if gi % 2 == 0 else "stldO",
                                sy.dma_start(
                                    out=s.istage[:, (gi % 2) * HID:
                                                 (gi % 2) * HID + row],
                                    in_=src[ch * 128:(ch + 1) * 128, :]), 16)
                        if ch % ch2 == ch2 - 1:
                            q = ch // ch2
                            sm.wait("stcast", sy, gi + 1)
                            if wi + q > 0:
                                sm.wait("stout", sy, 16 * (NGR * wi + q))
                            sm.post("sttp", sy.dma_start(
                                out=s.tpb[:].rearrange("q (m l) -> q m l",
                                                       l=128),
                                in_=s.nat[:], transpose=True), 16)
                sy.wait_ge(sm.ent("sttp")[0], sm.val("sttp"))

            def stage_casts(sc, todo):
                done = 0
                while stage_casts.gi < 2 * NHC + NDC and done < todo:
                    gi = stage_casts.gi
                    wi = 0 if gi < NHC else (1 if gi < 2 * NHC else 2)
                    ch = gi - WCUM[wi]
                    nm, nchunk, row, ch2 = STW[wi]
                    if ch % ch2 == 0 and gi > 0:
                        sm.wait("sttp", sc, 16 * (NGR * wi + ch // ch2))
                    sm.wait("stldE" if gi % 2 == 0 else "stldO", sc,
                            16 * (gi // 2 + 1))
                    sm.post("stcast", sc.copy(
                        s.nat[:, (ch % ch2) * row:(ch % ch2) * row + row],
                        s.istage[:, (gi % 2) * HID:(gi % 2) * HID + row]))
                    stage_casts.gi += 1
                    done += 1
            stage_casts.gi = 0

            @blk.scalar
            def _(sc):
                sm.wait("cst_rdy", sc)
                sm.wait("x0ld", sc)
                for tt in range(NT):
                    if tt == 0:
                        for dc in range(NDC):
                            i = sc.square(s.xsq[:, dc * 512:(dc + 1) * 512],
                                          s.xres[:, dc * T: dc * T + 512])
                        sm.post("sq_rdy", i)
                    sm.wait("s1s2_ev", sc, tt + 1)
                    sm.post("m2_rdy", sc.square(s.rstdB[:], s.meanB[:]))
                    sm.wait("var_rdy", sc, tt + 1)
                    sm.post("sd_rdy", sc.activation(s.varB[:], s.varB[:],
                                                    AF.Sqrt, bias=s.cst[:, 0:1]))
                    if tt + 1 < NT:
                        sm.wait("sums_done", sc, tt + 1)
                        for dc in range(NDC):
                            i = sc.square(
                                s.xsq[:, dc * 512:(dc + 1) * 512],
                                s.xres[:, dc * T + (tt + 1) * 512:
                                       dc * T + (tt + 1) * 512 + 512])
                        sm.post("sq_rdy", i)
                    stage_casts(sc, 14 if tt < NT - 1 else 56)

            @blk.tensor
            def _(te):
                sm.wait("cst_rdy", te)

                def te_stats(tt2):
                    sm.wait("sq_rdy", te, tt2 + 1)
                    if tt2 > 0:
                        sm.wait("stats_done", te, tt2)
                    for dc in range(NDC):
                        xsl = s.xres[:, dc * T + tt2 * 512:
                                     dc * T + tt2 * 512 + 512]
                        te.matmul(p.pS1[:], s.ones128[:], xsl,
                                  start=(dc == 0), stop=(dc == NDC - 1))
                        i = te.matmul(p.pS2[:], s.ones128[:],
                                      s.xsq[:, dc * 512:(dc + 1) * 512],
                                      start=(dc == 0), stop=(dc == NDC - 1))
                    sm.post("sums_done", i)

                for tt in range(NT):
                    if tt == 0:
                        te_stats(0)
                        sm.wait("wvald", te, 16 * NDC)
                    sm.wait("xn_rdy", te, tt + 1)
                    for ec in range(NDC):
                        pa = p.pA0 if ec % 2 == 0 else p.pA1
                        if tt * NDC + ec >= 2:
                            sm.wait("evac_done", te, tt * NDC + ec - 1)
                        xb = (tt % 2) * NDC * 512
                        for dc in range(NDC):
                            i = te.matmul(pa[:],
                                          s.wva_sb[:, dc * C + ec * 128: dc * C + ec * 128 + 128],
                                          s.xn[:, xb + dc * 512:
                                               xb + (dc + 1) * 512],
                                          start=(dc == 0), stop=(dc == NDC - 1))
                        sm.post("proj_done", i)
                        if ec == 1 and tt + 1 < NT:
                            te_stats(tt + 1)

            def ve_arith(ve, tt2):
                sm.wait("sums_done", ve, tt2 + 1)
                ve.tensor_scalar(s.meanB[:], p.pS1[:], 1.0 / C, None, ALU.mult)
                sm.post("s1s2_ev", ve.tensor_scalar(s.varB[:], p.pS2[:],
                                                    1.0 / C, None, ALU.mult))
                sm.wait("m2_rdy", ve, tt2 + 1)
                sm.post("var_rdy", ve.tensor_sub(s.varB[:], s.varB[:],
                                                 s.rstdB[:]))
                sm.wait("sd_rdy", ve, tt2 + 1)
                sm.post("stats_done", ve.reciprocal(s.rstdB[:], s.varB[:]))

            def ve_xn(ve, tt2):
                xb = (tt2 % 2) * NDC * 512
                if tt2 > 1:
                    sm.wait("proj_done", ve, (tt2 - 1) * NDC)
                for dc in range(NDC):
                    xsl = s.xres[:, dc * T + tt2 * 512: dc * T + tt2 * 512 + 512]
                    ve.tensor_sub(s.xn[:, xb + dc * 512: xb + (dc + 1) * 512],
                                  xsl, s.meanB[:])
                for dc in range(NDC):
                    i = ve.tensor_mul(
                        s.xn[:, xb + dc * 512: xb + (dc + 1) * 512],
                        s.xn[:, xb + dc * 512: xb + (dc + 1) * 512],
                        s.rstdB[:])
                sm.post("xn_rdy", i)

            @blk.vector
            def _(ve):
                for tt in range(NT):
                    if tt == 0:
                        ve_arith(ve, 0)
                        ve_xn(ve, 0)
                    if tt + 1 < NT:
                        ve_arith(ve, tt + 1)
                        ve_xn(ve, tt + 1)
                    for ec in range(NDC):
                        pa = p.pA0 if ec % 2 == 0 else p.pA1
                        sm.wait("proj_done", ve, tt * NDC + ec + 1)
                        xsl = s.xres[:, ec * T + tt * 512: ec * T + tt * 512 + 512]
                        sm.post("evac_done", ve.scalar_tensor_tensor(
                            xsl, pa[:], s.bva_pp[:, ec:ec + 1], xsl,
                            ALU.add, ALU.add))
        p1_es.close()

    def emit_ffn(tag, T0):
        esF = ExitStack()
        sbufF, psumF = make_alloc(esF)
        xnh = sbufF(tag + "xnh", [128, NDC * 512], F16)
        xnl = sbufF(tag + "xnl", [128, NDC * 512], F16)
        hidh = sbufF(tag + "hidh", [128, NHC * 512], F16)
        hidl = sbufF(tag + "hidl", [128, NHC * 512], F16)
        gsil = sbufF(tag + "gsil", [128, 512])
        hf = sbufF(tag + "hf", [128, 512])
        tmpf = sbufF(tag + "tmpf", [128, 512])
        xnf = sbufF(tag + "xnf", [128, 512])
        xsq = sbufF(tag + "xsq", [128, NDC * 512])
        meanB = sbufF(tag + "meanB", [128, 512])
        rstdB = sbufF(tag + "rstdB", [128, 512])
        varB = sbufF(tag + "varB", [128, 512])
        wstrA = sbufF(tag + "wstrA", [128, GT * 128], F16)
        wstrB = sbufF(tag + "wstrB", [128, GT * 128], F16)
        pG = [psumF(tag + "pG0", [128, 512]), psumF(tag + "pG1", [128, 512])]
        pU = [psumF(tag + "pU0", [128, 512]), psumF(tag + "pU1", [128, 512])]
        pD = [psumF(tag + "pD0", [128, 512]), psumF(tag + "pD1", [128, 512])]
        pS1 = psumF(tag + "pS1", [128, 512])
        pS2 = psumF(tag + "pS2", [128, 512])
        NT = 4
        S3 = S127

        def group_seq():
            for tt in range(NT):
                for j in range(4):
                    yield d.wgT, j
                    yield d.wuT, j
                for j in range(4):
                    yield d.wdT, j

        def T_(n):
            return "f_" + n

        with nc.Block(tag) as blk:
            def consumed_wait(eng, gi):
                tt, r = divmod(gi, 12)
                if r < 8:
                    j, kind = r // 2, ("gq" if r % 2 == 0 else "uq")
                    sm.wait(T_(kind), eng, tt * NHC + 6 * j + 6)
                else:
                    j = r - 8
                    sm.wait(T_("dnq"), eng, tt * NDC + 2 * j + 2)

            def stream_loads(eng, parity):
                for gi0, (src, j) in enumerate(group_seq()):
                    gi = T0 * 12 + gi0
                    if gi % 2 != parity:
                        continue
                    if gi >= 2:
                        consumed_wait(eng, gi - 2)
                    buf = wstrA if gi % 2 == 0 else wstrB
                    sem = "wldE" if gi % 2 == 0 else "wldO"
                    sm.post(T_(sem), eng.dma_start(
                        out=buf[:],
                        in_=src[:, j * GT * 128:(j + 1) * GT * 128]), 16)
                sm.wait(T_("wldE" if parity == 0 else "wldO"), eng)

            @blk.sync
            def _(sy):
                stream_loads(sy, 0)

            @blk.gpsimd
            def _(gp):
                stream_loads(gp, 1)

            def sc_squares(sc, tt):
                gt = T0 + tt
                if gt > 0:
                    sm.wait(T_("sums_done"), sc, gt)
                for dc in range(NDC):
                    i = sc.square(xsq[:, dc * 512:(dc + 1) * 512],
                                  s.xres[:, dc * T + tt * 512: dc * T + tt * 512 + 512])
                sm.post(T_("sq_rdy"), i)

            @blk.scalar
            def _(sc):
                for tt in range(NT):
                    gt = T0 + tt
                    if tt == 0:
                        sc_squares(sc, tt)
                    sm.wait(T_("s1s2_ev"), sc, gt + 1)
                    sm.post(T_("m2_rdy"), sc.square(rstdB[:], meanB[:]))
                    sm.wait(T_("var_rdy"), sc, gt + 1)
                    sm.post(T_("sd_rdy"), sc.activation(varB[:], varB[:],
                                                        AF.Sqrt, bias=s.cst[:, 0:1]))
                    if tt + 1 < NT:
                        sc_squares(sc, tt + 1)
                    if DBG_FFN and tag == "f1_" and tt == 1:
                        sm.wait(T_("dbg2"), sc, 32)
                    if tt > 0:
                        sm.wait(T_("dnq"), sc, gt * NDC)
                    for hc in range(NHC):
                        g = gt * NHC + hc
                        sm.wait(T_("gq"), sc, g + 1)
                        sm.post(T_("pg_free"), sc.activation(
                            gsil[:], pG[g % 2][:], AF.Silu, scale=S127))
                        sm.wait(T_("hf_rdy"), sc, g + 1)
                        sm.post(T_("hh_rdy"), sc.copy(
                            hidh[:, hc * 512:(hc + 1) * 512], hf[:]))

            @blk.tensor
            def _(te):
                gi_next = [T0 * 12]

                def need_group(gi):
                    while gi_next[0] <= gi:
                        g2 = gi_next[0]
                        sm.wait(T_("wldE" if g2 % 2 == 0 else "wldO"), te,
                                16 * (g2 // 2 + 1))
                        gi_next[0] += 1

                def te_stats(tt2):
                    gt2 = T0 + tt2
                    sm.wait(T_("sq_rdy"), te, gt2 + 1)
                    if gt2 > 0:
                        sm.wait(T_("stats_done"), te, gt2)
                    for dc in range(NDC):
                        xsl = s.xres[:, dc * T + tt2 * 512:
                                     dc * T + tt2 * 512 + 512]
                        te.matmul(pS1[:], s.ones128[:], xsl,
                                  start=(dc == 0), stop=(dc == NDC - 1))
                        i = te.matmul(pS2[:], s.ones128[:],
                                      xsq[:, dc * 512:(dc + 1) * 512],
                                      start=(dc == 0), stop=(dc == NDC - 1))
                    sm.post(T_("sums_done"), i)

                for tt in range(NT):
                    gt = T0 + tt
                    if tt == 0:
                        te_stats(tt)
                    sm.wait(T_("xn_rdy"), te, gt + 1)
                    base = gt * 12
                    for hc in range(NHC):
                        g = gt * NHC + hc
                        ggi = base + (hc // 6) * 2
                        ugi = ggi + 1
                        if hc % 6 == 0:
                            need_group(ggi)
                            need_group(ugi)
                        if g >= 2:
                            sm.wait(T_("pg_free"), te, g - 1)
                        gbuf = wstrA if ggi % 2 == 0 else wstrB
                        for dc in range(NDC):
                            m = hc * 8 + dc
                            w = gbuf[:, (m % GT) * 128:(m % GT) * 128 + 128]
                            te.matmul(pG[g % 2][:], w, xnh[:, dc * 512:(dc + 1) * 512],
                                      start=(dc == 0), stop=False)
                            i = te.matmul(pG[g % 2][:], w, xnl[:, dc * 512:(dc + 1) * 512],
                                          start=False, stop=(dc == NDC - 1))
                        sm.post(T_("gq"), i)
                        if g >= 2:
                            sm.wait(T_("pu_free"), te, g - 1)
                        ubuf = wstrA if ugi % 2 == 0 else wstrB
                        for dc in range(NDC):
                            m = hc * 8 + dc
                            w = ubuf[:, (m % GT) * 128:(m % GT) * 128 + 128]
                            te.matmul(pU[g % 2][:], w, xnh[:, dc * 512:(dc + 1) * 512],
                                      start=(dc == 0), stop=False)
                            i = te.matmul(pU[g % 2][:], w, xnl[:, dc * 512:(dc + 1) * 512],
                                          start=False, stop=(dc == NDC - 1))
                        sm.post(T_("uq"), i)
                    if tt + 1 < NT:
                        te_stats(tt + 1)
                    sm.wait(T_("hid_rdy"), te, (gt + 1) * NHC)
                    for dc in range(NDC):
                        q = gt * NDC + dc
                        dgi = base + 8 + dc // 2
                        if dc % 2 == 0:
                            need_group(dgi)
                        if q >= 2:
                            sm.wait(T_("pd_free"), te, q - 1)
                        dbuf = wstrA if dgi % 2 == 0 else wstrB
                        for hc in range(NHC):
                            m = dc * NHC + hc
                            w = dbuf[:, (m % GT) * 128:(m % GT) * 128 + 128]
                            te.matmul(pD[q % 2][:], w, hidh[:, hc * 512:(hc + 1) * 512],
                                      start=(hc == 0), stop=False)
                            i = te.matmul(pD[q % 2][:], w, hidl[:, hc * 512:(hc + 1) * 512],
                                          start=False, stop=(hc == NHC - 1))
                        sm.post(T_("dnq"), i)

            if DBG_FFN and tag == "f1_":
                @blk.gpsimd
                def _(gp):
                    sm.wait(T_("xn_rdy"), gp, 1)
                    sm.post(T_("dbg1"), gp.dma_start(
                        out=d.dbgxn[:, 0:NDC * 512], in_=xnh[:]), 16)
                    sm.post(T_("dbg1"), gp.dma_start(
                        out=d.dbgxn[:, NDC * 512:], in_=xnl[:]), 16)
                    sm.wait(T_("hid_rdy"), gp, NHC)
                    sm.post(T_("dbg2"), gp.dma_start(
                        out=d.dbghid[:, 0:NHC * 512], in_=hidh[:]), 16)
                    sm.post(T_("dbg2"), gp.dma_start(
                        out=d.dbghid[:, NHC * 512:], in_=hidl[:]), 16)
                    sm.wait(T_("dbg1"), gp)
                    sm.wait(T_("dbg2"), gp)

            def ve_arith(ve, tt2):
                gt2 = T0 + tt2
                sm.wait(T_("sums_done"), ve, gt2 + 1)
                if DBG_FFN and tag == "f1_" and tt2 == 1:
                    sm.wait(T_("dbg1"), ve, 32)
                    sm.wait(T_("dbg2"), ve, 32)
                ve.tensor_scalar(meanB[:], pS1[:], 1.0 / C, None, ALU.mult)
                sm.post(T_("s1s2_ev"), ve.tensor_scalar(varB[:], pS2[:],
                                                        1.0 / C, None, ALU.mult))
                sm.wait(T_("m2_rdy"), ve, gt2 + 1)
                sm.post(T_("var_rdy"), ve.tensor_sub(varB[:], varB[:], rstdB[:]))
                sm.wait(T_("sd_rdy"), ve, gt2 + 1)
                sm.post(T_("stats_done"), ve.reciprocal(rstdB[:], varB[:]))

            def ve_xnsplit(ve, tt2):
                gt2 = T0 + tt2
                if gt2 > 0:
                    sm.wait(T_("uq"), ve, gt2 * NHC)
                for dc in range(NDC):
                    xsl = s.xres[:, dc * T + tt2 * 512: dc * T + tt2 * 512 + 512]
                    ve.tensor_sub(tmpf[:], xsl, meanB[:])
                    ve.tensor_mul(xnf[:], tmpf[:], rstdB[:])
                    ve.tensor_copy(xnh[:, dc * 512:(dc + 1) * 512], xnf[:])
                    i = ve.scalar_tensor_tensor(
                        xnl[:, dc * 512:(dc + 1) * 512], xnf[:], 1.0,
                        xnh[:, dc * 512:(dc + 1) * 512], ALU.mult, ALU.subtract)
                sm.post(T_("xn_rdy"), i)

            @blk.vector
            def _(ve):
                for tt in range(NT):
                    gt = T0 + tt
                    if tt == 0:
                        ve_arith(ve, tt)
                        ve_xnsplit(ve, tt)
                    if gt > 0:
                        sm.wait(T_("dnq"), ve, gt * NDC)
                    for hc in range(NHC):
                        g = gt * NHC + hc
                        sm.wait(T_("pg_free"), ve, g + 1)
                        sm.wait(T_("uq"), ve, g + 1)
                        i = ve.scalar_tensor_tensor(hf[:], gsil[:], S127,
                                                    pU[g % 2][:], ALU.mult, ALU.mult)
                        sm.post(T_("hf_rdy"), i)
                        sm.post(T_("pu_free"), ve.tensor_copy(tmpf[:, 0:1], hf[:, 0:1]))
                        sm.wait(T_("hh_rdy"), ve, g + 1)
                        sm.post(T_("hid_rdy"), ve.scalar_tensor_tensor(
                            hidl[:, hc * 512:(hc + 1) * 512], hf[:], 1.0,
                            hidh[:, hc * 512:(hc + 1) * 512], ALU.mult, ALU.subtract))
                    if tt + 1 < NT:
                        ve_arith(ve, tt + 1)
                        ve_xnsplit(ve, tt + 1)
                    for dc in range(NDC):
                        q = gt * NDC + dc
                        sm.wait(T_("dnq"), ve, q + 1)
                        xsl = s.xres[:, dc * T + tt * 512: dc * T + tt * 512 + 512]
                        sm.post(T_("pd_free"), ve.scalar_tensor_tensor(
                            xsl, pD[q % 2][:], S3, xsl, ALU.mult, ALU.add))
        esF.close()

    if "ffn1" in PHASES:
        emit_ffn("f1_", 0)

    def emit_attnb():
        esB = ExitStack()
        sbufB, psumB = make_alloc(esB)
        TW = 256
        NTT = T // TW
        P3LV = 3
        P3CH = 3
        NH = 16
        n3 = sbufB("b_n3", [128, 2048])
        xxB = sbufB("b_xxB", [128, 2048])
        mixP = [sbufB("b_mixA", [128, 2048]), sbufB("b_mixB", [128, 2048])]
        rB = sbufB("b_r", [128, 2048])
        kB = sbufB("b_k", [128, 2048])
        tmp = sbufB("b_tmp", [128, 2048])
        k2tm = sbufB("b_k2tm", [128, 2048])
        vtm = sbufB("b_vtm", [128, 2048])
        wtm = sbufB("b_wtm", [128, 2048])
        ytm = sbufB("b_ytm", [128, 1024])
        lamP = sbufB("b_lamP", [128, 1024])
        lamEP = sbufB("b_lamEP", [128, 1024])
        SB = sbufB("b_S", [128, 512])
        carry8 = sbufB("b_carry", [128, 8])
        meanB = sbufB("b_mean", [128, TW])
        rstdB = sbufB("b_rstd", [128, TW])
        varB = sbufB("b_var", [128, TW])
        loraA = sbufB("b_loraA", [128, TW])
        loraB2 = sbufB("b_loraB", [128, TW])
        loraC = sbufB("b_loraC", [128, TW])
        tanh65 = sbufB("b_tanh65", [128, TW])
        tanhg = sbufB("b_tanhg", [128, TW])
        emal = sbufB("b_emal", [128, 64])
        dtm = sbufB("b_dtm", [128, 32])
        gnA = sbufB("b_gnA", [128, 16])
        gnB = sbufB("b_gnB", [128, 16])
        gnC = sbufB("b_gnC", [128, 16])
        rm64 = sbufB("b_rm64", [128, 1])
        dbgc = sbufB("b_dbgc", [128, 8])
        sc1 = sbufB("b_sc1", [128, 1])
        sc2 = sbufB("b_sc2", [128, 1])
        w2aug = [sbufB("b_w2a", [128, 1024]), sbufB("b_w2b", [128, 1024]),
                 sbufB("b_w2c", [128, 1024])]
        mw1 = sbufB("b_mw1", [128, NDC * 160])
        dw1 = sbufB("b_dw1", [128, NDC * 64])
        dw2td = sbufB("b_dw2td", [128, 1024])
        gw1 = sbufB("b_gw1", [128, NDC * 64])
        gw2 = sbufB("b_gw2", [128, 1024])
        maapp = sbufB("b_maapp", [128, 48])
        lngb = sbufB("b_lngb", [128, 16])
        Ublk = sbufB("b_ublk", [128, 128])
        maskSU = sbufB("b_mask", [128, 512], F16)
        triI = sbufB("b_triI", [128, 128])
        triS = sbufB("b_triS", [128, 128])
        idn = sbufB("b_idn", [128, 128])
        wpl = sbufB("b_wpl", [128, 2 * 1024])
        pL1 = psumB("b_pL1", [128, 512])
        pL2 = psumB("b_pL2", [128, 512])
        pP0 = psumB("b_pP0", [128, 512])
        pP1 = psumB("b_pP1", [128, 512])
        pQ = psumB("b_pQ", [128, 1024])
        pW0 = psumB("b_pW0", [128, 512])
        pW1 = psumB("b_pW1", [128, 512])

        MIX_F = ["w", "k", "v", "r", "g"]
        PLANES = [d.Wk, d.Wv, d.Wr, d.Wo]
        PRJSLOT = [(pL1, 0), (pL1, 256), (pL2, 0), (pL2, 256),
                   (pQ, 0), (pQ, 256), (pQ, 512), (pQ, 768)]

        def xsl_(dc2, tt):
            return s.xres[:, dc2 * T + tt * TW: dc2 * T + tt * TW + TW]


        _tesig = [None]

        def sig_mm(te, sig):
            if _tesig[0] != sig:
                te.drain()
                _tesig[0] = sig

        def pairview(t_, pr):
            return bass.AP(t_, pr * 128, [[2048, 128], [256, 8], [1, 128]])

        def lamview(t_, off=0, base=0):
            return bass.AP(t_, base + off,
                           [[2048, 128], [128, 8], [64, 2], [1, 64]])

        def colX(t_, off, base=0):
            return bass.AP(t_, base + off,
                           [[2048, 128], [128, 8], [64, 2], [0, 64]])

        def eview(t_, base=0):
            return bass.AP(t_, base, [[2048, 128], [128, 8], [1, 128]])

        def SBp(h):
            return pW1[(h % 2) * 64:(h % 2) * 64 + 64,
                       (h // 2) * 64:(h // 2) * 64 + 64]

        def p3_scalar_tail(sc, tt):
            if P3LV < 1:
                return
            sm.wait("b_wtme", sc, 2 * tt + 2)
            sm.post("b_ew", sc.activation(wtm[:], wtm[:], AF.Exp))
            for pr in range(2):
                for j in range(3):
                    sm.wait("b_esub", sc, tt * 6 + pr * 3 + j + 1)
                    sm.post("b_eexp", sc.activation(
                        mixP[1][:, 0:1024], mixP[1][:, 0:1024], AF.Exp))
                sm.wait("b_tpe", sc, tt * 4 + pr * 2 + 2)
                for ch2 in range(2):
                    ch = pr * 2 + ch2
                    sc.activation(emal[:, ch * 8:ch * 8 + 8],
                                  bass.AP(mixP[0], ch2 * 64 + 31,
                                          [[2048, 128], [128, 8]]),
                                  AF.Exp, scale=-1.0)
                    i = sc.activation(emal[:, 32 + ch * 8:32 + ch * 8 + 8],
                                      bass.AP(mixP[0], ch2 * 64 + 63,
                                              [[2048, 128], [128, 8]]),
                                      AF.Exp, scale=-1.0)
                sm.post("b_emal", i)
            if P3LV < 3:
                return
            for pr in range(2):
                sm.wait("b_ysum", sc, tt * 4 + pr * 2 + 2)
                sm.post("b_ysq", sc.square(mixP[1][:, 1024:2048], ytm[:]))
                sm.wait("b_gnv", sc, tt * 2 + pr + 1)
                sm.post("b_gnsd", sc.activation(
                    gnB[:], gnB[:], AF.Sqrt, bias=s.cst[:, 1:2]))

        def p3_te_tail(te, tt):
            if P3LV < 1:
                return
            sm.wait("b_rk", te, tt + 1)
            if tt > 0:
                sm.wait("b_zslot", te, tt * 16)
            sig_mm(te, ("mm", 0, 128))
            for dc2 in range(NDC):
                i = te.matmul(pW1[0:16, 0:TW], Ublk[:, dc2 * 16:(dc2 + 1) * 16],
                              tmp[:, dc2 * TW:(dc2 + 1) * TW],
                              start=(dc2 == 0), stop=(dc2 == NDC - 1))
            sm.post("b_dq", i)
            sm.wait("b_dst", te, tt + 1)
            sm.wait("b_gl", te, 2 * tt + 2)
            if tt > 0:
                sm.wait("b_zslot", te, tt * 16)
            sig_mm(te, ("tp", 0, 16))
            te.transpose(pW0[:, 0:16], tmp[0:16, 0:128], idn[0:16, 0:16])
            sm.post("b_dtp", te.transpose(pW0[:, 16:32], tmp[0:16, 128:256],
                                          idn[0:16, 0:16]))
            for pr in range(2):
                P = tt * 2 + pr
                sm.wait("b_ew", te, tt + 1)
                if P == 0:
                    sm.wait("b_tri", te, 1)
                if P > 0:
                    sm.wait("b_k2e", te, 2 * P)
                if pr == 0 and tt > 0 and P3CH >= 2:
                    sm.wait("b_ysum", te, tt * 4)
                sig_mm(te, ("mm", 0, 128))
                te.matmul(pQ[:, 0:512], triI[:],
                          wtm[:, pr * 1024:pr * 1024 + 512],
                          start=True, stop=True)
                i = te.matmul(pQ[:, 512:1024], triI[:],
                              wtm[:, pr * 1024 + 512:(pr + 1) * 1024],
                              start=True, stop=True)
                sm.post("b_lq", i)
                sm.wait("b_lev", te, 4 * P + 1)
                sig_mm(te, ("mm", 0, 128))
                te.matmul(pQ[:, 0:512], triS[:],
                          wtm[:, pr * 1024:pr * 1024 + 512],
                          start=True, stop=True)
                i = te.matmul(pQ[:, 512:1024], triS[:],
                              wtm[:, pr * 1024 + 512:(pr + 1) * 1024],
                              start=True, stop=True)
                sm.post("b_lq", i)
                sm.wait("b_lev", te, 4 * P + 2)
                for grp in range(2):
                    gI = P * 2 + grp
                    if grp == 0:
                        sm.wait("b_dev", te, tt + 1)
                    if gI > 0:
                        sm.wait("b_tpe", te, gI)
                    sig_mm(te, ("tp", 0, 128))
                    for q in range(4):
                        pc = grp * 4 + q
                        te.transpose(pW0[:, q * 128:(q + 1) * 128],
                                     lamP[:, pc * 128:(pc + 1) * 128], idn[:])
                        i = te.transpose(pW1[:, q * 128:(q + 1) * 128],
                                         lamEP[:, pc * 128:(pc + 1) * 128],
                                         idn[:])
                    sm.post("b_tpq", i)
                sm.wait("b_k2f", te, P + 1)
                sm.wait("b_lev", te, 4 * P + 4)
                for grp in range(2):
                    sig_mm(te, ("tp", 0, 128))
                    for q in range(4):
                        pc = grp * 4 + q
                        i = te.transpose(pQ[:, q * 128:(q + 1) * 128] if grp == 0
                                         else pQ[:, 512 + q * 128:
                                                 512 + (q + 1) * 128],
                                         bass.AP(tmp, pr * 1024 + pc * 128,
                                                 [[2048, 128], [1, 128]]),
                                         idn[:])
                    sm.post("b_k2q", i)
            if P3LV < 2:
                return
            for pr in range(2):
                P = tt * 2 + pr
                for cc in range(2):
                    ch = pr * 2 + cc
                    sm.wait("b_fac", te, tt * 6 + pr * 3 + 3)
                    sm.wait("b_csb", te, tt * 4 + ch)
                    if P3CH >= 3:
                        sm.wait("b_supd", te, tt * 4 + ch)
                    if cc == 0 and P3LV >= 3:
                        sm.wait("b_yhev", te, P * 2)
                        sm.wait("b_zslot", te, P * 8)
                    if ch == 0:
                        sm.wait("b_tpe", te, (tt * 2 + 2) * 2)
                        sm.wait("b_dev", te, tt + 1)
                    for h in [0, 2, 4, 6, 8, 10, 12, 14, 1, 3, 5, 7, 9, 11, 13, 15]:
                        sig_mm(te, ("mm", (h % 2) * 64, 64))
                        kv = bass.AP(kB, (h // 2) * 256 + pr * 128 + cc * 64 +
                                     ((h % 2) * 64) * 2048, [[2048, 64], [1, 64]])
                        rv = bass.AP(rB, (h // 2) * 256 + pr * 128 + cc * 64 +
                                     ((h % 2) * 64) * 2048, [[2048, 64], [1, 64]])
                        if h < 8:
                            out = pW0[cc * 64:cc * 64 + 64, h * 64:h * 64 + 64]
                        else:
                            out = pW1[cc * 64:cc * 64 + 64,
                                      (h - 8) * 64:(h - 8) * 64 + 64]
                        i = te.matmul(out, kv, rv, start=True, stop=True,
                                      skip_group_check=True)
                    sm.post("b_gq", i)
                    if P3CH < 2:
                        continue
                    sm.wait("b_csb", te, tt * 4 + ch + 1)
                    if P3CH >= 2:
                        sm.wait("b_ysum", te, tt * 4 + ch)
                    sig_mm(te, ("mm", cc * 64, 64))
                    for h in range(NH):
                        cs = bass.AP(wtm, h * 64 + (cc * 64) * 2048,
                                     [[2048, 64], [1, 64]])
                        vv = bass.AP(vtm, pr * 1024 + h * 64 + (cc * 64) * 2048,
                                     [[2048, 64], [1, 64]])
                        i = te.matmul(pQ[cc * 64:cc * 64 + 64,
                                         h * 64:h * 64 + 64],
                                      cs, vv, start=True, stop=True,
                                      skip_group_check=True)
                    sm.post("b_yint", i)
                    sm.wait("b_ssclv", te, tt * 4 + ch + 1)
                    sm.wait("b_yie", te, tt * 4 + ch + 1)
                    for h in [0, 2, 4, 6, 8, 10, 12, 14, 1, 3, 5, 7, 9, 11, 13, 15]:
                        sig_mm(te, ("mm", (h % 2) * 64, 64))
                        ss = bass.AP(wtm, 1024 + (h // 2) * 64 +
                                     ((h % 2) * 64) * 2048, [[2048, 64], [1, 64]])
                        rv = bass.AP(rB, (h // 2) * 256 + pr * 128 + cc * 64 +
                                     ((h % 2) * 64) * 2048, [[2048, 64], [1, 64]])
                        i = te.matmul(pQ[cc * 64:cc * 64 + 64,
                                         h * 64:h * 64 + 64],
                                      rv, ss, start=True, stop=True,
                                      skip_group_check=True)
                    sm.post("b_yq", i)
                    if P3CH < 3:
                        continue
                    sig_mm(te, ("mm", cc * 64, 64))
                    for h in range(NH):
                        k2 = bass.AP(k2tm, pr * 1024 + h * 64 + (cc * 64) * 2048,
                                     [[2048, 64], [1, 64]])
                        vv = bass.AP(vtm, pr * 1024 + h * 64 + (cc * 64) * 2048,
                                     [[2048, 64], [1, 64]])
                        i = te.matmul(SBp(h), k2, vv, start=True, stop=True,
                                      skip_group_check=True)
                    sm.post("b_incq", i)
                if P3LV < 3:
                    continue
                sm.wait("b_gnap", te, P + 1)
                sm.wait("b_csb", te, tt * 4 + (pr + 1) * 2)
                for grp in range(2):
                    if P * 2 + grp > 0:
                        sm.wait("b_yhev", te, P * 2 + grp)
                    sig_mm(te, ("tp", 0, 128))
                    for q in range(4):
                        jc = grp * 4 + q
                        i = te.transpose(pW0[:, q * 128:(q + 1) * 128],
                                         ytm[:, jc * 128:(jc + 1) * 128], idn[:])
                    sm.post("b_yhq", i)
                sm.wait("b_supd", te, tt * 4 + (pr + 1) * 2)
                sm.wait("b_yhev", te, P * 2 + 2)
                sig_mm(te, ("mm", 0, 64))
                for jc in range(NDC):
                    if P * 8 + jc >= 2:
                        sm.wait("b_zslot", te, P * 8 + jc - 1)
                    gb = pW0 if jc % 2 == 0 else pW1
                    gcol = ((jc // 2) % 2) * 128
                    sm.post("b_zgq", te.matmul(
                        gb[:, gcol:gcol + 128],
                        gw2[0:64, jc * 128:(jc + 1) * 128],
                        tanhg[0:64, pr * 128:(pr + 1) * 128],
                        start=True, stop=True))
            if P3LV < 3:
                return
            sm.wait("b_zslot", te, tt * 16 + 16)
            sm.wait("b_ysum", te, tt * 4 + 4)
            if tt > 0:
                sm.wait("b_xev", te, tt * 8)
            sig_mm(te, ("mm", 0, 128))
            for dc2 in range(NDC):
                gpl = tt * 32 + 24 + dc2
                sm.wait("b_wpldE" if gpl % 2 == 0 else "b_wpldO", te,
                                16 * (gpl // 2 + 1))
                pw = wpl[:, (gpl % 2) * 1024:(gpl % 2 + 1) * 1024]
                for ec in range(NDC):
                    pt, pcol = PRJSLOT[ec]
                    i = te.matmul(pt[:, pcol:pcol + TW],
                                  pw[:, ec * 128:(ec + 1) * 128],
                                  bass.AP(n3, dc2 * 128,
                                          [[2048, 128], [1024, 2], [1, 128]]),
                                  start=(dc2 == 0 and ec % 2 == 0),
                                  stop=(dc2 == NDC - 1 and ec % 2 == 1))
                sm.post("b_wod", i)

        def p3_vec_tail(ve, tt):
            if P3LV < 1:
                return
            sm.post("b_rk", ve.tensor_mul(tmp[:], rB[:], kB[:]))
            sm.wait("b_dq", ve, tt + 1)
            sm.post("b_dst", ve.tensor_copy(tmp[0:16, 0:TW], pW1[0:16, 0:TW]))
            sm.wait("b_dtp", ve, tt + 1)
            ve.tensor_copy(dtm[:, 0:16], pW0[:, 0:16])
            sm.post("b_dev", ve.tensor_copy(dtm[:, 16:32], pW0[:, 16:32]))
            for pr in range(2):
                P = tt * 2 + pr
                sm.wait("b_lq", ve, 2 * P + 1)
                sm.post("b_lev", ve.tensor_copy(lamP[:], pQ[:]))
                sm.wait("b_lq", ve, 2 * P + 2)
                i = ve.tensor_copy(lamEP[:], pQ[:])
                sm.post("b_lev", i, 3)
                for grp in range(2):
                    sm.wait("b_tpq", ve, P * 2 + grp + 1)
                    ve.tensor_copy(mixP[0][:, grp * 512:(grp + 1) * 512], pW0[:])
                    sm.post("b_tpe", ve.tensor_copy(
                        mixP[0][:, 1024 + grp * 512:1024 + (grp + 1) * 512],
                        pW1[:]))
                sm.post("b_esub", ve.tensor_tensor(
                    lamview(mixP[1]), lamview(mixP[0]), colX(mixP[0], 63),
                    ALU.subtract))
                sm.wait("b_eexp", ve, tt * 6 + pr * 3 + 1)
                sm.post("b_k2f", ve.tensor_tensor(
                    eview(tmp, pr * 1024), pairview(kB, pr), eview(mixP[1]),
                    ALU.mult))
                sm.post("b_esub", ve.tensor_tensor(
                    lamview(mixP[1]), lamview(mixP[0]), colX(mixP[0], 31),
                    ALU.subtract))
                sm.wait("b_eexp", ve, tt * 6 + pr * 3 + 2)
                ve.tensor_tensor(pairview(kB, pr), pairview(kB, pr),
                                 eview(mixP[1]), ALU.mult)
                sm.post("b_esub", ve.tensor_tensor(
                    lamview(mixP[1]), colX(mixP[0], 31),
                    lamview(mixP[0], 0, 1024), ALU.subtract))
                sm.wait("b_eexp", ve, tt * 6 + pr * 3 + 3)
                sm.post("b_fac", ve.tensor_tensor(
                    pairview(rB, pr), pairview(rB, pr), eview(mixP[1]),
                    ALU.mult), 3)
                for grp in range(2):
                    sm.wait("b_k2q", ve, P * 2 + grp + 1)
                    sm.post("b_k2e", ve.tensor_copy(
                        k2tm[:, pr * 1024 + grp * 512:
                             pr * 1024 + (grp + 1) * 512],
                        pQ[:, 0:512] if grp == 0 else pQ[:, 512:1024]))
            if P3LV < 2:
                return
            for pr in range(2):
                P = tt * 2 + pr
                for cc in range(2):
                    ch = pr * 2 + cc
                    sm.wait("b_gq", ve, tt * 4 + ch + 1)
                    ve.tensor_tensor(
                        bass.AP(wtm, (cc * 64) * 2048, [[2048, 64], [1, 512]]),
                        pW0[cc * 64:cc * 64 + 64, :],
                        maskSU[cc * 64:cc * 64 + 64, :], ALU.mult)
                    sm.post("b_csb", ve.tensor_tensor(
                        bass.AP(wtm, 512 + (cc * 64) * 2048,
                                [[2048, 64], [1, 512]]),
                        pW1[cc * 64:cc * 64 + 64, :],
                        maskSU[cc * 64:cc * 64 + 64, :], ALU.mult))
                    sm.wait("b_emal", ve, tt * 2 + pr + 1)
                    if P3CH < 2:
                        continue
                    sm.post("b_ssclv", ve.tensor_tensor(
                        wtm[:, 1024:1536].rearrange("q (a b) -> q a b", a=8),
                        SB[:].rearrange("q (a b) -> q a b", a=8),
                        bass.AP(emal, ch * 8, [[64, 128], [1, 8], [0, 64]]),
                        ALU.mult))
                    if cc == 0:
                        if P3LV >= 3:
                            sm.wait("b_yhq", ve, (tt * 2 + pr) * 2)
                        sm.wait("b_dev", ve, tt + 1)
                        ve.tensor_tensor(
                            ytm[:].rearrange("q (a b) -> q a b", a=16),
                            vtm[:, pr * 1024:(pr + 1) * 1024].rearrange(
                                "q (a b) -> q a b", a=16),
                            bass.AP(dtm, pr * 16, [[32, 128], [1, 16], [0, 64]]),
                            ALU.mult)
                    sm.wait("b_yint", ve, tt * 4 + ch + 1)
                    if DBG_P3 and tt == 0 and ch == 0:
                        ve.tensor_copy(mixP[1][:, 1024:2048], pQ[:])
                    sm.post("b_yie", ve.tensor_add(
                        bass.AP(ytm, (cc * 64) * 1024, [[1024, 64], [1, 1024]]),
                        bass.AP(ytm, (cc * 64) * 1024, [[1024, 64], [1, 1024]]),
                        pQ[cc * 64:cc * 64 + 64, :]))
                    sm.wait("b_yq", ve, tt * 4 + ch + 1)
                    sm.post("b_ysum", ve.tensor_add(
                        bass.AP(ytm, (cc * 64) * 1024, [[1024, 64], [1, 1024]]),
                        bass.AP(ytm, (cc * 64) * 1024, [[1024, 64], [1, 1024]]),
                        pQ[cc * 64:cc * 64 + 64, :]))
                    if P3CH < 3:
                        continue
                    sm.wait("b_incq", ve, tt * 4 + ch + 1)
                    ve.tensor_tensor(SB[:].rearrange("q (a b) -> q a b", a=8),
                                     SB[:].rearrange("q (a b) -> q a b", a=8),
                                     bass.AP(emal, 32 + ch * 8,
                                             [[64, 128], [1, 8], [0, 64]]),
                                     ALU.mult)
                    sm.post("b_supd", ve.tensor_add(SB[:], SB[:], pW1[:]))
                if P3CH < 2:
                    continue
                if P3LV < 3:
                    continue
                sm.wait("b_ysq", ve, tt * 2 + pr + 1)
                ve.tensor_reduce(gnA[:],
                                 ytm[:].rearrange("q (a b) -> q a b", a=16),
                                 mybir.AxisListType.X, ALU.add)
                sm.post("b_gns", ve.tensor_reduce(
                    gnB[:],
                    mixP[1][:, 1024:2048].rearrange("q (a b) -> q a b", a=16),
                    mybir.AxisListType.X, ALU.add))
                sm.wait("b_gns", ve)
                ve.tensor_scalar(gnA[:], gnA[:], 1.0 / 64, None, ALU.mult)
                sm.post("b_gns", ve.tensor_scalar(
                    gnB[:], gnB[:], 1.0 / 64, None, ALU.mult))
                sm.wait("b_gns", ve)
                sm.post("b_gns", ve.tensor_tensor(gnC[:], gnA[:], gnA[:],
                                                  ALU.mult))
                sm.wait("b_gns", ve)
                sm.post("b_gnv", ve.tensor_sub(gnB[:], gnB[:], gnC[:]))
                sm.wait("b_gnsd", ve, tt * 2 + pr + 1)
                sm.post("b_gns", ve.reciprocal(gnC[:], gnB[:]))
                ve.tensor_tensor(ytm[:].rearrange("q (a b) -> q a b", a=16),
                                 ytm[:].rearrange("q (a b) -> q a b", a=16),
                                 bass.AP(gnA, 0, [[16, 128], [1, 16], [0, 64]]),
                                 ALU.subtract)
                sm.wait("b_gns", ve)
                sm.post("b_gnap", ve.tensor_tensor(
                    ytm[:].rearrange("q (a b) -> q a b", a=16),
                    ytm[:].rearrange("q (a b) -> q a b", a=16),
                    bass.AP(gnC, 0, [[16, 128], [1, 16], [0, 64]]),
                    ALU.mult))
                for grp in range(2):
                    sm.wait("b_yhq", ve, P * 2 + grp + 1)
                    sm.post("b_yhev", ve.tensor_copy(
                        n3[:, pr * 1024 + grp * 512:pr * 1024 + (grp + 1) * 512],
                        pW0[:]))
                for jc in range(NDC):
                    sm.wait("b_zgq", ve, P * 8 + jc + 1)
                    sm.post("b_gns", ve.tensor_scalar(
                        varB[:, 0:128],
                        n3[:, pr * 1024 + jc * 128:
                           pr * 1024 + (jc + 1) * 128],
                        lngb[:, jc:jc + 1], lngb[:, 8 + jc:9 + jc],
                        ALU.mult, ALU.add))
                    sm.wait("b_gns", ve)
                    gb = pW0 if jc % 2 == 0 else pW1
                    gcol = ((jc // 2) % 2) * 128
                    sm.post("b_zslot", ve.tensor_tensor(
                        n3[:, pr * 1024 + jc * 128:pr * 1024 + (jc + 1) * 128],
                        varB[:, 0:128], gb[:, gcol:gcol + 128], ALU.mult))
                    sm.wait("b_zslot", ve)
            if P3LV < 3:
                return
            sm.wait("b_wod", ve, tt * 8 + 8)
            for ec in range(NDC):
                pt, pcol = PRJSLOT[ec]
                xsl2 = s.xres[:, ec * T + tt * TW: ec * T + tt * TW + TW]
                sm.post("b_xev", ve.scalar_tensor_tensor(
                    xsl2, pt[:, pcol:pcol + TW], 1.0, xsl2,
                    ALU.mult, ALU.add))
            sm.post("b_zdone", ve.tensor_copy(varB[:, 0:1], meanB[:, 0:1]))
        with nc.Block("p3") as blk:
            @blk.gpsimd
            def _(gp):
                gp.memset(carry8[:], 0.0)
                gp.memset(SB[:], 0.0)
                for t_ in (loraA, loraB2):
                    gp.memset(t_[32:33, :], 1.0)
                    gp.memset(t_[96:97, :], 1.0)
                gp.memset(loraC[32:33, :], 1.0)
                gp.memset(tanh65[64:65, :], 1.0)
                gp.memset(Ublk[:], 0.0)
                gp.iota(tmp[:, 0:512], [[0, 8], [1, 64]], channel_multiplier=0,
                        allow_small_or_imprecise_dtypes=True)
                gp.iota(ytm[:, 0:128], [[1, 128]], channel_multiplier=0,
                        allow_small_or_imprecise_dtypes=True)
                gp.iota(rm64[0:64, :], [[0, 1]], channel_multiplier=1,
                        allow_small_or_imprecise_dtypes=True)
                gp.iota(rm64[64:128, :], [[0, 1]], channel_multiplier=1,
                        allow_small_or_imprecise_dtypes=True)
                gp.memset(sc1[0:64, :], 63.0)
                gp.memset(sc1[64:128, :], 127.0)
                for dc2 in range(NDC):
                    sm.post("b_swld", gp.dma_start(
                        out=mw1[:, dc2 * 160:(dc2 + 1) * 160],
                        in_=d.maa_w1[dc2 * 128:(dc2 + 1) * 128, :]), 16)
                    sm.post("b_swld", gp.dma_start(
                        out=dw1[:, dc2 * 64:(dc2 + 1) * 64],
                        in_=d.decay_w1[dc2 * 128:(dc2 + 1) * 128, :]), 16)
                    sm.post("b_swld", gp.dma_start(
                        out=gw1[:, dc2 * 64:(dc2 + 1) * 64],
                        in_=d.gate_w1[dc2 * 128:(dc2 + 1) * 128, :]), 16)
                sm.post("b_swld", gp.dma_start(out=dw2td[0:64, :],
                                               in_=d.decay_w2[:, :]), 16)
                sm.post("b_swld", gp.dma_start(
                    out=dw2td[64:65, :],
                    in_=bass.AP(d.time_decay, 0, [[C, 1], [1, C]])), 16)
                sm.post("b_swld", gp.dma_start(out=gw2[0:64, :],
                                               in_=d.gate_w2[:, :]), 16)
                for fi, (t_, base) in enumerate([(w2aug[0], 0), (w2aug[0], 64),
                                                 (w2aug[1], 0), (w2aug[1], 64),
                                                 (w2aug[2], 0)]):
                    sm.post("b_swld", gp.dma_start(out=t_[base:base + 32, :],
                                                   in_=d.maa_w2[fi, :, :]), 16)
                for fi, nm in enumerate(["maa_x", "maa_w", "maa_k",
                                         "maa_v", "maa_r", "maa_g"]):
                    sm.post("b_swld", gp.dma_start(
                        out=maapp[:, fi * 8:(fi + 1) * 8],
                        in_=bass.AP(getattr(d, nm), 0, [[1, 128], [128, NDC]]),
                        allow_slow_non_contiguous=True), 16)
                for fi, nm in enumerate(["ln_g", "ln_b"]):
                    sm.post("b_swld", gp.dma_start(
                        out=lngb[:, fi * 8:(fi + 1) * 8],
                        in_=bass.AP(getattr(d, nm), 0, [[1, 128], [128, NDC]]),
                        allow_slow_non_contiguous=True), 16)
                for fi2, (t_, base, nm) in enumerate([
                        (w2aug[0], 32, "maa_w"), (w2aug[0], 96, "maa_k"),
                        (w2aug[1], 32, "maa_v"), (w2aug[1], 96, "maa_r"),
                        (w2aug[2], 32, "maa_g")]):
                    sm.post("b_swld", gp.dma_start(
                        out=t_[base:base + 1, :],
                        in_=bass.AP(getattr(d, nm), 0, [[C, 1], [1, C]])), 16)
                for h in range(NH):
                    half = (h % 2) * 64
                    sm.post("b_swld", gp.dma_start(
                        out=Ublk[half:half + 64,
                                 (h // 2) * 16 + h:(h // 2) * 16 + h + 1],
                        in_=bass.AP(d.u, h * 64, [[1, 64], [64, 1]])), 16)
                sm.wait("b_swld", gp)

            @blk.vector
            def _(ve):
                sm.wait("b_swld", ve)
                ve.tensor_copy(dbgc[:, 0:1], rm64[:])
                sm.post("b_mask", ve.tensor_scalar(
                    maskSU[:], tmp[:, 0:512], rm64[:], None, ALU.is_gt))
                ve.tensor_scalar(idn[:], ytm[:, 0:128], s.rowv[:], None,
                                 ALU.is_equal)
                ve.tensor_scalar(triI[:], ytm[:, 0:128], s.rowv[:], None,
                                 ALU.is_ge)
                ve.tensor_scalar(tmp[:, 0:128], ytm[:, 0:128], sc1[:],
                                 None, ALU.is_le)
                ve.tensor_mul(triI[:], triI[:], tmp[:, 0:128])
                ve.tensor_scalar(triS[:], ytm[:, 0:128], s.rowv[:], None,
                                 ALU.is_gt)
                sm.post("b_tri", ve.tensor_mul(triS[:], triS[:], tmp[:, 0:128]))

            @blk.sync
            def _(sy):
                def plane_consumed(sy, p):
                    tp_, j = p // 32, p % 32
                    if j < 24:
                        sm.wait("b_prj", sy, tp_ * 24 + j + 1)
                    else:
                        sm.wait("b_wod", sy, tp_ * 8 + (j - 24) + 1)

                NPL = 32 if P3LV >= 3 else 24
                for g0 in range(NTT * NPL):
                    g = (g0 // NPL) * 32 + (g0 % NPL)
                    if g0 >= 2:
                        plane_consumed(sy, ((g0 - 2) // NPL) * 32 + ((g0 - 2) % NPL))
                    w, dc2 = PLANES[(g % 32) // 8], g % 8
                    sm.post("b_wpldE" if g % 2 == 0 else "b_wpldO",
                            sy.dma_start(
                                out=wpl[:, (g % 2) * 1024:(g % 2 + 1) * 1024],
                                in_=w[dc2 * 128:(dc2 + 1) * 128, :]), 16)
                sm.wait("b_wpldE", sy)
                sm.wait("b_wpldO", sy)

            @blk.scalar
            def _(sc):
                for tt in range(NTT):
                    if tt > 0:
                        sm.wait("b_zdone", sc, tt)
                    for dc2 in range(NDC):
                        i = sc.square(tmp[:, dc2 * TW:(dc2 + 1) * TW],
                                      xsl_(dc2, tt))
                    sm.post("b_sq", i)
                    sm.wait("b_s12", sc, tt + 1)
                    sm.post("b_m2", sc.square(rstdB[:], meanB[:]))
                    sm.wait("b_var", sc, tt + 1)
                    sm.post("b_sd", sc.activation(varB[:], varB[:], AF.Sqrt,
                                                  bias=s.cst[:, 0:1]))
                    sm.wait("b_lmm", sc, 2 * (tt + 1))
                    sc.activation(loraA[0:32, :], pP0[0:32, 0:TW], AF.Tanh)
                    sc.activation(loraA[64:96, :], pP0[32:64, 0:TW], AF.Tanh)
                    sc.activation(loraB2[0:32, :], pP0[64:96, 0:TW], AF.Tanh)
                    sc.activation(loraB2[64:96, :], pP0[96:128, 0:TW], AF.Tanh)
                    sm.post("b_tanh", sc.activation(loraC[0:32, :],
                                                    pP1[0:32, 0:TW], AF.Tanh))
                    sm.wait("b_wl", sc, 2 * tt + 1)
                    sm.post("b_wl", sc.activation(tanh65[0:64, :],
                                                  pW0[0:64, 0:TW], AF.Tanh))
                    sm.wait("b_gl", sc, 2 * tt + 1)
                    sm.post("b_gl", sc.activation(tanhg[0:64, :],
                                                  pW0[64:128, 0:TW], AF.Tanh))
                    p3_scalar_tail(sc, tt)

            def emit_proj(te, tt, fi):
                f = MIX_F[fi]
                sm.wait("b_mixv", te, tt * 40 + fi * 8 + 8)
                mx = mixP[fi % 2]
                if f == "w":
                    if tt > 0:
                        sm.wait("b_wl", te, 2 * tt)
                        sm.wait("b_gl", te, 2 * tt)
                    sig_mm(te, ("mm", 0, 128))
                    for dc2 in range(NDC):
                        i = te.matmul(pW0[0:64, 0:TW],
                                      dw1[:, dc2 * 64:(dc2 + 1) * 64],
                                      mx[:, dc2 * TW:(dc2 + 1) * TW],
                                      start=(dc2 == 0), stop=(dc2 == NDC - 1))
                    sm.post("b_wl", i)
                    sm.wait("b_wl", te, 2 * tt + 2)
                    sm.wait("b_pev", te, tt * 20)
                    if tt > 0:
                        sm.wait("b_xev", te, tt * 8)
                    sig_mm(te, ("mm", 0, 65))
                    for tc in range(2):
                        if tc == 1:
                            sm.wait("b_wtme", te, 2 * tt + 1)
                        te.matmul(pQ[:, 0:512],
                                  tanh65[0:65, tc * 128:(tc + 1) * 128],
                                  dw2td[0:65, 0:512], start=True, stop=True)
                        i = te.matmul(pQ[:, 512:1024],
                                      tanh65[0:65, tc * 128:(tc + 1) * 128],
                                      dw2td[0:65, 512:1024], start=True, stop=True)
                        sm.post("b_wtm", i)
                elif f == "g":
                    sig_mm(te, ("mm", 0, 128))
                    for dc2 in range(NDC):
                        i = te.matmul(pW0[64:128, 0:TW],
                                      gw1[:, dc2 * 64:(dc2 + 1) * 64],
                                      mx[:, dc2 * TW:(dc2 + 1) * TW],
                                      start=(dc2 == 0), stop=(dc2 == NDC - 1))
                    sm.post("b_gl", i)
                elif f == "v":
                    sm.wait("b_pev", te, tt * 20 + 8)
                    sig_mm(te, ("mm", 0, 128))
                    for dc2 in range(NDC):
                        gpl = tt * 32 + 8 + dc2
                        sm.wait("b_wpldE" if gpl % 2 == 0 else "b_wpldO", te,
                                16 * (gpl // 2 + 1))
                        pw = wpl[:, (gpl % 2) * 1024:(gpl % 2 + 1) * 1024]
                        st, sp = (dc2 == 0), (dc2 == NDC - 1)
                        te.matmul(pL1[:, 0:512],
                                  mx[:, dc2 * TW: dc2 * TW + 128],
                                  pw[:, 0:512], start=st, stop=sp)
                        te.matmul(pL2[:, 0:512],
                                  mx[:, dc2 * TW: dc2 * TW + 128],
                                  pw[:, 512:1024], start=st, stop=sp)
                        te.matmul(pQ[:, 0:512],
                                  mx[:, dc2 * TW + 128: dc2 * TW + 256],
                                  pw[:, 0:512], start=st, stop=sp)
                        i = te.matmul(pQ[:, 512:1024],
                                      mx[:, dc2 * TW + 128: dc2 * TW + 256],
                                      pw[:, 512:1024], start=st, stop=sp)
                        sm.post("b_prj", i)
                else:
                    base_pl = tt * 32 + (0 if f == "k" else 16)
                    if f == "k":
                        sm.wait("b_pev", te, tt * 20)
                        sm.wait("b_wtme", te, 2 * tt + 2)
                        if tt > 0:
                            sm.wait("b_xev", te, tt * 8)
                    else:
                        sm.wait("b_pev", te, tt * 20 + 12)
                    sig_mm(te, ("mm", 0, 128))
                    for dc2 in range(NDC):
                        gpl = base_pl + dc2
                        sm.wait("b_wpldE" if gpl % 2 == 0 else "b_wpldO", te,
                                16 * (gpl // 2 + 1))
                        pw = wpl[:, (gpl % 2) * 1024:(gpl % 2 + 1) * 1024]
                        for ec in range(NDC):
                            pt, pcol = PRJSLOT[ec]
                            i = te.matmul(pt[:, pcol:pcol + TW],
                                          pw[:, ec * 128:(ec + 1) * 128],
                                          mx[:, dc2 * TW:(dc2 + 1) * TW],
                                          start=(dc2 == 0 and ec % 2 == 0),
                                          stop=(dc2 == NDC - 1 and ec % 2 == 1))
                        sm.post("b_prj", i)

            @blk.tensor
            def _(te):
                for tt in range(NTT):
                    sm.wait("b_sq", te, tt + 1)
                    if tt > 0:
                        sm.wait("b_stats", te, tt)
                        sm.wait("b_pev", te, (tt - 1) * 20 + 16)
                        sm.wait("b_xev", te, tt * 8)
                    sig_mm(te, ("mm", 0, 128))
                    for dc2 in range(NDC):
                        te.matmul(pL1[:, 0:TW], s.ones128[:], xsl_(dc2, tt),
                                  start=(dc2 == 0), stop=(dc2 == NDC - 1))
                        i = te.matmul(pL2[:, 0:TW], s.ones128[:],
                                      tmp[:, dc2 * TW:(dc2 + 1) * TW],
                                      start=(dc2 == 0), stop=(dc2 == NDC - 1))
                    sm.post("b_sums", i)
                    sm.wait("b_xxx", te, tt + 1)
                    if tt > 0:
                        sm.wait("b_mixv", te, tt * 40)
                    sig_mm(te, ("mm", 0, 128))
                    for dc2 in range(NDC):
                        i = te.matmul(pP0[:, 0:TW],
                                      mw1[:, dc2 * 160: dc2 * 160 + 128],
                                      mixP[0][:, dc2 * TW:(dc2 + 1) * TW],
                                      start=(dc2 == 0), stop=(dc2 == NDC - 1))
                    sm.post("b_lmm", i)
                    for dc2 in range(NDC):
                        i = te.matmul(pP1[0:32, 0:TW],
                                      mw1[:, dc2 * 160 + 128: dc2 * 160 + 160],
                                      mixP[0][:, dc2 * TW:(dc2 + 1) * TW],
                                      start=(dc2 == 0), stop=(dc2 == NDC - 1))
                    sm.post("b_lmm", i)
                    for fi in range(5):
                        sm.wait("b_tanh", te, tt + 1)
                        lt = [loraA, loraA, loraB2, loraB2, loraC][fi]
                        lb = [0, 64, 0, 64, 0][fi]
                        wt = w2aug[fi // 2]
                        sig_mm(te, ("mm", lb, 33))
                        for ec in range(NDC):
                            slot = (pP0 if ec % 2 == 0 else pP1)
                            scol = 256 if fi % 2 == 1 else 0
                            gmm = tt * 40 + fi * 8 + ec
                            if gmm >= 2:
                                sm.wait("b_mixv", te, gmm - 1)
                            sm.post("b_mm", te.matmul(
                                slot[:, scol:scol + TW],
                                wt[lb:lb + 33, ec * 128:(ec + 1) * 128],
                                lt[lb:lb + 33, :], start=True, stop=True))
                        emit_proj(te, tt, fi)
                    p3_te_tail(te, tt)

            @blk.vector
            def _(ve):
                for tt in range(NTT):
                    sm.wait("b_sums", ve, tt + 1)
                    ve.tensor_scalar(meanB[:], pL1[:, 0:TW], 1.0 / C, None,
                                     ALU.mult)
                    sm.post("b_s12", ve.tensor_scalar(varB[:], pL2[:, 0:TW],
                                                      1.0 / C, None, ALU.mult))
                    sm.wait("b_m2", ve, tt + 1)
                    sm.post("b_var", ve.tensor_sub(varB[:], varB[:], rstdB[:]))
                    sm.wait("b_sd", ve, tt + 1)
                    sm.post("b_stats", ve.reciprocal(rstdB[:], varB[:]))
                    if tt > 0:
                        sm.wait("b_zdone", ve, tt)
                    for dc2 in range(NDC):
                        ve.tensor_sub(n3[:, dc2 * TW:(dc2 + 1) * TW],
                                      xsl_(dc2, tt), meanB[:])
                        ve.tensor_mul(n3[:, dc2 * TW:(dc2 + 1) * TW],
                                      n3[:, dc2 * TW:(dc2 + 1) * TW], rstdB[:])
                    ve.tensor_tensor(
                        bass.AP(xxB, 1, [[2048, 128], [TW, NDC], [1, TW - 1]]),
                        bass.AP(n3, 0, [[2048, 128], [TW, NDC], [1, TW - 1]]),
                        bass.AP(n3, 1, [[2048, 128], [TW, NDC], [1, TW - 1]]),
                        ALU.subtract)
                    i = ve.tensor_tensor(
                        bass.AP(xxB, 0, [[2048, 128], [TW, NDC]]),
                        carry8[:],
                        bass.AP(n3, 0, [[2048, 128], [TW, NDC]]),
                        ALU.subtract)
                    sm.post("b_xx", i)
                    sm.post("b_carry", ve.tensor_copy(
                        carry8[:], bass.AP(n3, TW - 1, [[2048, 128], [TW, NDC]])))
                    for dc2 in range(NDC):
                        i = ve.scalar_tensor_tensor(
                            mixP[0][:, dc2 * TW:(dc2 + 1) * TW],
                            xxB[:, dc2 * TW:(dc2 + 1) * TW],
                            maapp[:, dc2:dc2 + 1],
                            n3[:, dc2 * TW:(dc2 + 1) * TW], ALU.mult, ALU.add)
                    sm.post("b_xxx", i)
                    for fi in range(5):
                        f = MIX_F[fi]
                        mx = mixP[fi % 2]
                        for ec in range(NDC):
                            slot = (pP0 if ec % 2 == 0 else pP1)
                            scol = 256 if fi % 2 == 1 else 0
                            gmm = tt * 40 + fi * 8 + ec
                            sm.wait("b_mm", ve, gmm + 1)
                            ve.tensor_tensor(varB[:],
                                             xxB[:, ec * TW:(ec + 1) * TW],
                                             slot[:, scol:scol + TW], ALU.mult)
                            i = ve.tensor_add(mx[:, ec * TW:(ec + 1) * TW],
                                              varB[:],
                                              n3[:, ec * TW:(ec + 1) * TW])
                            sm.post("b_mixv", i)
                        if f == "w":
                            for tc in range(2):
                                sm.wait("b_wtm", ve, 2 * tt + tc + 1)
                                sm.post("b_wtme", ve.tensor_copy(
                                    wtm[:, tc * 1024:(tc + 1) * 1024], pQ[:]))
                        elif f in ("k", "r"):
                            dst = kB if f == "k" else rB
                            sm.wait("b_prj", ve,
                                    tt * 24 + (8 if f == "k" else 24))
                            for ec in range(NDC):
                                pt, pcol = PRJSLOT[ec]
                                sm.post("b_pev", ve.tensor_copy(
                                    dst[:, ec * TW:(ec + 1) * TW],
                                    pt[:, pcol:pcol + TW]))
                        elif f == "v":
                            sm.wait("b_prj", ve, tt * 24 + 16)
                            sm.post("b_pev", ve.tensor_copy(
                                vtm[:, 0:512], pL1[:, 0:512]))
                            sm.post("b_pev", ve.tensor_copy(
                                vtm[:, 512:1024], pL2[:, 0:512]))
                            sm.post("b_pev", ve.tensor_copy(
                                vtm[:, 1024:1536], pQ[:, 0:512]))
                            sm.post("b_pev", ve.tensor_copy(
                                vtm[:, 1536:2048], pQ[:, 512:1024]))
                    p3_vec_tail(ve, tt)

            if DBG_P3:
                @blk.gpsimd
                def _(gp):
                    sm.wait("b_zdone", gp, NTT)
                    off = 0
                    for t_ in [n3, rB, kB, k2tm, vtm, wtm, tmp]:
                        sm.post("b_dbg", gp.dma_start(
                            out=d.dbg3[:, off:off + 2048], in_=t_[:]), 16)
                        off += 2048
                    for t_, w_ in [(SB, 512), (emal, 64), (dtm, 32),
                                   (ytm, 1024), (lamP, 1024), (lamEP, 1024),
                                   (mixP[0], 2048), (triI, 128), (triS, 128),
                                   (idn, 128), (rm64, 1), (dbgc, 8),
                                   (mixP[1], 2048), (gnA, 16), (gnB, 16),
                                   (gnC, 16)]:
                        sm.post("b_dbg", gp.dma_start(
                            out=d.dbg3[:, off:off + w_], in_=t_[:],
                            allow_slow_non_contiguous=True), 16)
                        off += w_
                    sm.wait("b_dbg", gp)

        esB.close()

    if "attn_b" in PHASES:
        emit_attnb()

    if "ffn2" in PHASES:
        emit_ffn("f2_", 4)

    with nc.Block("pout") as blk:
        @blk.gpsimd
        def _(gp):
            for dc in range(NDC):
                sm.post("outst", gp.dma_start(
                    out=d.out[dc * 128:(dc + 1) * 128, :],
                    in_=s.xres[:, dc * T:(dc + 1) * T]), 16)
            sm.wait("outst", gp)

    return nc


_NC_CACHE = None


def _get_nc():
    global _NC_CACHE
    if _NC_CACHE is None:
        _NC_CACHE = build_nc()
    return _NC_CACHE


def kernel(**inputs):
    nc = _get_nc()
    xf = np.ascontiguousarray(np.asarray(inputs["L0_attn_a_X"], dtype=np.float32))
    maps = []
    for b in range(8):
        m = {
            "x0": np.ascontiguousarray(xf[b].T),
            "wva": np.asarray(inputs["L0_attn_a_Wv"], np.float32),
            "bva": np.asarray(inputs["L0_attn_a_bv"], np.float32),
            "wgi": np.asarray(inputs["L0_ffn_w_gate"], np.int32),
            "wui": np.asarray(inputs["L0_ffn_w_up"], np.int32),
            "wdi": np.asarray(inputs["L0_ffn_w_down"], np.int32),
        }
        for n in ["maa_x", "maa_w", "maa_k", "maa_v", "maa_r", "maa_g",
                  "maa_w1", "maa_w2", "Wr", "Wk", "Wv", "Wo", "gate_w1",
                  "gate_w2", "time_decay", "decay_w1", "decay_w2", "u",
                  "ln_g", "ln_b"]:
            m[n] = np.asarray(inputs["L0_attn_b_" + n], np.float32)
        maps.append(m)
    res = run_bass_kernel_spmd(nc, maps, core_ids=list(range(8)))
    out = np.stack([np.ascontiguousarray(r["out"].T) for r in res.results])
    return out




# version 12
# speedup vs baseline: 1.0235x; 1.0235x over previous
from contextlib import ExitStack

import numpy as np
import concourse.bass as bass
import concourse.mybir as mybir
from concourse.bass_utils import run_bass_kernel_spmd

F32 = mybir.dt.float32
F16 = mybir.dt.float16
I32 = mybir.dt.int32
AF = mybir.ActivationFunctionType
ALU = mybir.AluOpType

T = 2048
C = 1024
NDC = C // 128
HID = 3072
NHC = HID // 128
S127 = 1.0 / 127.0
EPS_LN = 1e-5
EPS_GN = 0.00064

PHASES = ("attn_a", "ffn1", "attn_b", "ffn2")
DBG_FFN = False


class Sems:

    def __init__(self, nc, es):
        object.__setattr__(self, "nc", nc)
        object.__setattr__(self, "es", es)
        object.__setattr__(self, "m", {})

    def ent(self, name):
        if name not in self.m:
            h = self.es.enter_context(self.nc.semaphore(name))
            self.m[name] = [h, 0]
        return self.m[name]

    def post(self, name, instr, n=1):
        e = self.ent(name)
        instr.then_inc(e[0], n)
        e[1] += n
        return e[1]

    def val(self, name):
        return self.ent(name)[1]

    def wait(self, name, eng, v=None):
        e = self.ent(name)
        eng.wait_ge(e[0], e[1] if v is None else v)


class B:
    pass


def build_nc(detect_races=False):
    nc = bass.Bass("TRN2", target_bir_lowering=False,
                   detect_race_conditions=detect_races)
    es = ExitStack()
    nc._keepalive = es

    d = B()
    s = B()
    p = B()

    d.x0 = nc.dram_tensor("x0", [C, T], F32, kind="ExternalInput")
    d.wva = nc.dram_tensor("wva", [C, C], F32, kind="ExternalInput")
    d.bva = nc.dram_tensor("bva", [C], F32, kind="ExternalInput")
    d.wgi = nc.dram_tensor("wgi", [HID, C], I32, kind="ExternalInput")
    d.wui = nc.dram_tensor("wui", [HID, C], I32, kind="ExternalInput")
    d.wdi = nc.dram_tensor("wdi", [C, HID], I32, kind="ExternalInput")
    for n, shp in [("maa_x", [C]), ("maa_w", [C]), ("maa_k", [C]),
                   ("maa_v", [C]), ("maa_r", [C]), ("maa_g", [C]),
                   ("maa_w1", [C, 160]), ("maa_w2", [5, 32, C]),
                   ("Wr", [C, C]), ("Wk", [C, C]), ("Wv", [C, C]),
                   ("Wo", [C, C]), ("gate_w1", [C, 64]), ("gate_w2", [64, C]),
                   ("time_decay", [C]), ("decay_w1", [C, 64]),
                   ("decay_w2", [64, C]), ("u", [16, 64]),
                   ("ln_g", [C]), ("ln_b", [C])]:
        setattr(d, n, nc.dram_tensor(n, shp, F32, kind="ExternalInput"))
    d.out = nc.dram_tensor("out", [C, T], F32, kind="ExternalOutput")
    DBG_P3 = False
    if DBG_P3:
        d.dbg3 = nc.dram_tensor("dbg3", [128, 7 * 2048 + 512 + 64 + 32 + 1024 + 4096 + 2048 + 3 * 128 + 1 + 8 + 2048 + 48],
                                F32, kind="ExternalOutput")
    if DBG_FFN:
        d.dbgxn = nc.dram_tensor("dbgxn", [128, 2 * NDC * 512], F16, kind="ExternalOutput")
        d.dbghid = nc.dram_tensor("dbghid", [128, 2 * NHC * 512], F16, kind="ExternalOutput")
    d.wgT = nc.dram_tensor("wgT_scr", [128, NHC * NDC * 128], F16)
    d.wuT = nc.dram_tensor("wuT_scr", [128, NHC * NDC * 128], F16)
    d.wdT = nc.dram_tensor("wdT_scr", [128, NDC * NHC * 128], F16)

    def make_alloc(stack):
        def sbuf(name, shape, dt=F32):
            t = stack.enter_context(nc.sbuf_tensor(name, shape, dt))
            setattr(s, name, t)
            return t

        def psum(name, shape, dt=F32):
            t = stack.enter_context(nc.psum_tensor(name, shape, dt))
            setattr(p, name, t)
            return t
        return sbuf, psum

    sbuf, psum = make_alloc(es)

    GT = 48
    sbuf("xres", [128, NDC * T])
    sbuf("ones128", [128, 128])
    sbuf("rowv", [128, 1])
    sbuf("bva_pp", [128, NDC])
    sbuf("cst", [128, 4])

    sm = Sems(nc, es)

    HCOLS = 6 * C
    NGR = 4
    STW = [("wgi", NHC, C, 6), ("wui", NHC, C, 6), ("wdi", NDC, HID, 2)]
    WCUM = [0, NHC, 2 * NHC]
    if "attn_a" in PHASES:
        p1_es = ExitStack()
        sbuf1, psum1 = make_alloc(p1_es)
        sbuf1("wva_sb", [128, NDC * C])
        sbuf1("xsq", [128, NDC * 512])
        sbuf1("meanB", [128, 512])
        sbuf1("rstdB", [128, 512])
        sbuf1("varB", [128, 512])
        sbuf1("xn", [128, 2 * NDC * 512])
        sbuf1("nat", [128, HCOLS], F16)
        sbuf1("tpb", [128, HCOLS], F16)
        sbuf1("istage", [128, 2 * HID], I32)
        psum1("pS1", [128, 512])
        psum1("pS2", [128, 512])
        psum1("pA0", [128, 512])
        psum1("pA1", [128, 512])
        with nc.Block("p1") as blk:
            NT = 4

            @blk.gpsimd
            def _(gp):
                gp.iota(s.rowv[:], [[0, 1]], channel_multiplier=1,
                        allow_small_or_imprecise_dtypes=True)
                gp.memset(s.ones128[:], 1.0)
                gp.memset(s.cst[:, 0:1], EPS_LN)
                sm.post("cst_rdy", gp.memset(s.cst[:, 1:2], EPS_GN))
                for dc in range(NDC):
                    sm.post("x0ld", gp.dma_start(
                        out=s.xres[:, dc * T:(dc + 1) * T],
                        in_=d.x0[dc * 128:(dc + 1) * 128, :]), 16)
                sm.post("x0ld", gp.dma_start(
                    out=s.bva_pp[:],
                    in_=bass.AP(d.bva, 0, [[1, 128], [128, NDC]]),
                    allow_slow_non_contiguous=True), 16)
                for dc in range(NDC):
                    sm.post("wvald", gp.dma_start(
                        out=s.wva_sb[:, dc * C:(dc + 1) * C],
                        in_=d.wva[dc * 128:(dc + 1) * 128, :]), 16)
                for wi, dst in enumerate([d.wgT, d.wuT, d.wdT]):
                    for q in range(NGR):
                        sm.wait("sttp", gp, 16 * (NGR * wi + q + 1))
                        sm.post("stout", gp.dma_start(
                            out=dst[:, q * HCOLS:(q + 1) * HCOLS],
                            in_=s.tpb[:]), 16)
                sm.wait("stout", gp)
                sm.wait("x0ld", gp)
                sm.wait("wvald", gp)
                sm.wait("stldE", gp)
                sm.wait("stldO", gp)

            @blk.sync
            def _(sy):
                for wi, (nm, nchunk, row, ch2) in enumerate(STW):
                    src = getattr(d, nm)
                    for ch in range(nchunk):
                        gi = WCUM[wi] + ch
                        if gi > 1:
                            sm.wait("stcast", sy, gi - 1)
                        sm.post("stldE" if gi % 2 == 0 else "stldO",
                                sy.dma_start(
                                    out=s.istage[:, (gi % 2) * HID:
                                                 (gi % 2) * HID + row],
                                    in_=src[ch * 128:(ch + 1) * 128, :]), 16)
                        if ch % ch2 == ch2 - 1:
                            q = ch // ch2
                            sm.wait("stcast", sy, gi + 1)
                            if wi + q > 0:
                                sm.wait("stout", sy, 16 * (NGR * wi + q))
                            sm.post("sttp", sy.dma_start(
                                out=s.tpb[:].rearrange("q (m l) -> q m l",
                                                       l=128),
                                in_=s.nat[:], transpose=True), 16)
                sy.wait_ge(sm.ent("sttp")[0], sm.val("sttp"))

            def stage_casts(sc, todo):
                done = 0
                while stage_casts.gi < 2 * NHC + NDC and done < todo:
                    gi = stage_casts.gi
                    wi = 0 if gi < NHC else (1 if gi < 2 * NHC else 2)
                    ch = gi - WCUM[wi]
                    nm, nchunk, row, ch2 = STW[wi]
                    if ch % ch2 == 0 and gi > 0:
                        sm.wait("sttp", sc, 16 * (NGR * wi + ch // ch2))
                    sm.wait("stldE" if gi % 2 == 0 else "stldO", sc,
                            16 * (gi // 2 + 1))
                    sm.post("stcast", sc.copy(
                        s.nat[:, (ch % ch2) * row:(ch % ch2) * row + row],
                        s.istage[:, (gi % 2) * HID:(gi % 2) * HID + row]))
                    stage_casts.gi += 1
                    done += 1
            stage_casts.gi = 0

            @blk.scalar
            def _(sc):
                sm.wait("cst_rdy", sc)
                sm.wait("x0ld", sc)
                for tt in range(NT):
                    if tt == 0:
                        for dc in range(NDC):
                            i = sc.square(s.xsq[:, dc * 512:(dc + 1) * 512],
                                          s.xres[:, dc * T: dc * T + 512])
                        sm.post("sq_rdy", i)
                    sm.wait("s1s2_ev", sc, tt + 1)
                    sm.post("m2_rdy", sc.square(s.rstdB[:], s.meanB[:]))
                    sm.wait("var_rdy", sc, tt + 1)
                    sm.post("sd_rdy", sc.activation(s.varB[:], s.varB[:],
                                                    AF.Sqrt, bias=s.cst[:, 0:1]))
                    if tt + 1 < NT:
                        sm.wait("sums_done", sc, tt + 1)
                        for dc in range(NDC):
                            i = sc.square(
                                s.xsq[:, dc * 512:(dc + 1) * 512],
                                s.xres[:, dc * T + (tt + 1) * 512:
                                       dc * T + (tt + 1) * 512 + 512])
                        sm.post("sq_rdy", i)
                    stage_casts(sc, 14 if tt < NT - 1 else 56)

            @blk.tensor
            def _(te):
                sm.wait("cst_rdy", te)

                def te_stats(tt2):
                    sm.wait("sq_rdy", te, tt2 + 1)
                    if tt2 > 0:
                        sm.wait("stats_done", te, tt2)
                    for dc in range(NDC):
                        xsl = s.xres[:, dc * T + tt2 * 512:
                                     dc * T + tt2 * 512 + 512]
                        te.matmul(p.pS1[:], s.ones128[:], xsl,
                                  start=(dc == 0), stop=(dc == NDC - 1))
                        i = te.matmul(p.pS2[:], s.ones128[:],
                                      s.xsq[:, dc * 512:(dc + 1) * 512],
                                      start=(dc == 0), stop=(dc == NDC - 1))
                    sm.post("sums_done", i)

                for tt in range(NT):
                    if tt == 0:
                        te_stats(0)
                        sm.wait("wvald", te, 16 * NDC)
                    sm.wait("xn_rdy", te, tt + 1)
                    for ec in range(NDC):
                        pa = p.pA0 if ec % 2 == 0 else p.pA1
                        if tt * NDC + ec >= 2:
                            sm.wait("evac_done", te, tt * NDC + ec - 1)
                        xb = (tt % 2) * NDC * 512
                        for dc in range(NDC):
                            i = te.matmul(pa[:],
                                          s.wva_sb[:, dc * C + ec * 128: dc * C + ec * 128 + 128],
                                          s.xn[:, xb + dc * 512:
                                               xb + (dc + 1) * 512],
                                          start=(dc == 0), stop=(dc == NDC - 1))
                        sm.post("proj_done", i)
                        if ec == 1 and tt + 1 < NT:
                            te_stats(tt + 1)

            def ve_arith(ve, tt2):
                sm.wait("sums_done", ve, tt2 + 1)
                ve.tensor_scalar(s.meanB[:], p.pS1[:], 1.0 / C, None, ALU.mult)
                sm.post("s1s2_ev", ve.tensor_scalar(s.varB[:], p.pS2[:],
                                                    1.0 / C, None, ALU.mult))
                sm.wait("m2_rdy", ve, tt2 + 1)
                sm.post("var_rdy", ve.tensor_sub(s.varB[:], s.varB[:],
                                                 s.rstdB[:]))
                sm.wait("sd_rdy", ve, tt2 + 1)
                sm.post("stats_done", ve.reciprocal(s.rstdB[:], s.varB[:]))

            def ve_xn(ve, tt2):
                xb = (tt2 % 2) * NDC * 512
                if tt2 > 1:
                    sm.wait("proj_done", ve, (tt2 - 1) * NDC)
                for dc in range(NDC):
                    xsl = s.xres[:, dc * T + tt2 * 512: dc * T + tt2 * 512 + 512]
                    ve.tensor_sub(s.xn[:, xb + dc * 512: xb + (dc + 1) * 512],
                                  xsl, s.meanB[:])
                for dc in range(NDC):
                    i = ve.tensor_mul(
                        s.xn[:, xb + dc * 512: xb + (dc + 1) * 512],
                        s.xn[:, xb + dc * 512: xb + (dc + 1) * 512],
                        s.rstdB[:])
                sm.post("xn_rdy", i)

            @blk.vector
            def _(ve):
                for tt in range(NT):
                    if tt == 0:
                        ve_arith(ve, 0)
                        ve_xn(ve, 0)
                    if tt + 1 < NT:
                        ve_arith(ve, tt + 1)
                        ve_xn(ve, tt + 1)
                    for ec in range(NDC):
                        pa = p.pA0 if ec % 2 == 0 else p.pA1
                        sm.wait("proj_done", ve, tt * NDC + ec + 1)
                        xsl = s.xres[:, ec * T + tt * 512: ec * T + tt * 512 + 512]
                        sm.post("evac_done", ve.scalar_tensor_tensor(
                            xsl, pa[:], s.bva_pp[:, ec:ec + 1], xsl,
                            ALU.add, ALU.add))
        p1_es.close()

    def emit_ffn(tag, T0):
        esF = ExitStack()
        sbufF, psumF = make_alloc(esF)
        xnh = sbufF(tag + "xnh", [128, NDC * 512], F16)
        xnl = sbufF(tag + "xnl", [128, NDC * 512], F16)
        hidh = sbufF(tag + "hidh", [128, NHC * 512], F16)
        hidl = sbufF(tag + "hidl", [128, NHC * 512], F16)
        gsil = sbufF(tag + "gsil", [128, 512])
        hf = sbufF(tag + "hf", [128, 512])
        tmpf = sbufF(tag + "tmpf", [128, 512])
        xnf = sbufF(tag + "xnf", [128, 512])
        xsq = sbufF(tag + "xsq", [128, NDC * 512])
        meanB = sbufF(tag + "meanB", [128, 512])
        rstdB = sbufF(tag + "rstdB", [128, 512])
        varB = sbufF(tag + "varB", [128, 512])
        wstrA = sbufF(tag + "wstrA", [128, GT * 128], F16)
        wstrB = sbufF(tag + "wstrB", [128, GT * 128], F16)
        pG = [psumF(tag + "pG0", [128, 512]), psumF(tag + "pG1", [128, 512])]
        pU = [psumF(tag + "pU0", [128, 512]), psumF(tag + "pU1", [128, 512])]
        pD = [psumF(tag + "pD0", [128, 512]), psumF(tag + "pD1", [128, 512])]
        pS1 = psumF(tag + "pS1", [128, 512])
        pS2 = psumF(tag + "pS2", [128, 512])
        NT = 4
        S3 = S127

        def group_seq():
            for tt in range(NT):
                for j in range(4):
                    yield d.wgT, j
                    yield d.wuT, j
                for j in range(4):
                    yield d.wdT, j

        def T_(n):
            return "f_" + n

        with nc.Block(tag) as blk:
            def consumed_wait(eng, gi):
                tt, r = divmod(gi, 12)
                if r < 8:
                    j, kind = r // 2, ("gq" if r % 2 == 0 else "uq")
                    sm.wait(T_(kind), eng, tt * NHC + 6 * j + 6)
                else:
                    j = r - 8
                    sm.wait(T_("dnq"), eng, tt * NDC + 2 * j + 2)

            def stream_loads(eng, parity):
                for gi0, (src, j) in enumerate(group_seq()):
                    gi = T0 * 12 + gi0
                    if gi % 2 != parity:
                        continue
                    if gi >= 2:
                        consumed_wait(eng, gi - 2)
                    buf = wstrA if gi % 2 == 0 else wstrB
                    sem = "wldE" if gi % 2 == 0 else "wldO"
                    sm.post(T_(sem), eng.dma_start(
                        out=buf[:],
                        in_=src[:, j * GT * 128:(j + 1) * GT * 128]), 16)
                sm.wait(T_("wldE" if parity == 0 else "wldO"), eng)

            @blk.sync
            def _(sy):
                stream_loads(sy, 0)

            @blk.gpsimd
            def _(gp):
                stream_loads(gp, 1)
                if T0 == 4:
                    for tt in range(NT):
                        for dc in range(NDC):
                            sm.wait("f_pd_free", gp, (T0 + tt) * NDC + dc + 1)
                            sm.post("outst", gp.dma_start(
                                out=d.out[dc * 128:(dc + 1) * 128,
                                          tt * 512:(tt + 1) * 512],
                                in_=s.xres[:, dc * T + tt * 512:
                                           dc * T + tt * 512 + 512]), 16)

            def sc_squares(sc, tt):
                gt = T0 + tt
                if gt > 0:
                    sm.wait(T_("sums_done"), sc, gt)
                for dc in range(NDC):
                    i = sc.square(xsq[:, dc * 512:(dc + 1) * 512],
                                  s.xres[:, dc * T + tt * 512: dc * T + tt * 512 + 512])
                sm.post(T_("sq_rdy"), i)

            @blk.scalar
            def _(sc):
                for tt in range(NT):
                    gt = T0 + tt
                    if tt == 0:
                        sc_squares(sc, tt)
                    sm.wait(T_("s1s2_ev"), sc, gt + 1)
                    sm.post(T_("m2_rdy"), sc.square(rstdB[:], meanB[:]))
                    sm.wait(T_("var_rdy"), sc, gt + 1)
                    sm.post(T_("sd_rdy"), sc.activation(varB[:], varB[:],
                                                        AF.Sqrt, bias=s.cst[:, 0:1]))
                    if tt + 1 < NT:
                        sc_squares(sc, tt + 1)
                    if DBG_FFN and tag == "f1_" and tt == 1:
                        sm.wait(T_("dbg2"), sc, 32)
                    if tt > 0:
                        sm.wait(T_("dnq"), sc, gt * NDC)
                    for hc in range(NHC):
                        g = gt * NHC + hc
                        sm.wait(T_("gq"), sc, g + 1)
                        sm.post(T_("pg_free"), sc.activation(
                            gsil[:], pG[g % 2][:], AF.Silu, scale=S127))
                        sm.wait(T_("hf_rdy"), sc, g + 1)
                        sm.post(T_("hh_rdy"), sc.copy(
                            hidh[:, hc * 512:(hc + 1) * 512], hf[:]))

            @blk.tensor
            def _(te):
                gi_next = [T0 * 12]

                def need_group(gi):
                    while gi_next[0] <= gi:
                        g2 = gi_next[0]
                        sm.wait(T_("wldE" if g2 % 2 == 0 else "wldO"), te,
                                16 * (g2 // 2 + 1))
                        gi_next[0] += 1

                def te_stats(tt2):
                    gt2 = T0 + tt2
                    sm.wait(T_("sq_rdy"), te, gt2 + 1)
                    if gt2 > 0:
                        sm.wait(T_("stats_done"), te, gt2)
                    for dc in range(NDC):
                        xsl = s.xres[:, dc * T + tt2 * 512:
                                     dc * T + tt2 * 512 + 512]
                        te.matmul(pS1[:], s.ones128[:], xsl,
                                  start=(dc == 0), stop=(dc == NDC - 1))
                        i = te.matmul(pS2[:], s.ones128[:],
                                      xsq[:, dc * 512:(dc + 1) * 512],
                                      start=(dc == 0), stop=(dc == NDC - 1))
                    sm.post(T_("sums_done"), i)

                for tt in range(NT):
                    gt = T0 + tt
                    if tt == 0:
                        te_stats(tt)
                    sm.wait(T_("xn_rdy"), te, gt + 1)
                    base = gt * 12
                    for hc in range(NHC):
                        g = gt * NHC + hc
                        ggi = base + (hc // 6) * 2
                        ugi = ggi + 1
                        if hc % 6 == 0:
                            need_group(ggi)
                            need_group(ugi)
                        if g >= 2:
                            sm.wait(T_("pg_free"), te, g - 1)
                        gbuf = wstrA if ggi % 2 == 0 else wstrB
                        for dc in range(NDC):
                            m = hc * 8 + dc
                            w = gbuf[:, (m % GT) * 128:(m % GT) * 128 + 128]
                            te.matmul(pG[g % 2][:], w, xnh[:, dc * 512:(dc + 1) * 512],
                                      start=(dc == 0), stop=False)
                            i = te.matmul(pG[g % 2][:], w, xnl[:, dc * 512:(dc + 1) * 512],
                                          start=False, stop=(dc == NDC - 1))
                        sm.post(T_("gq"), i)
                        if g >= 2:
                            sm.wait(T_("pu_free"), te, g - 1)
                        ubuf = wstrA if ugi % 2 == 0 else wstrB
                        for dc in range(NDC):
                            m = hc * 8 + dc
                            w = ubuf[:, (m % GT) * 128:(m % GT) * 128 + 128]
                            te.matmul(pU[g % 2][:], w, xnh[:, dc * 512:(dc + 1) * 512],
                                      start=(dc == 0), stop=False)
                            i = te.matmul(pU[g % 2][:], w, xnl[:, dc * 512:(dc + 1) * 512],
                                          start=False, stop=(dc == NDC - 1))
                        sm.post(T_("uq"), i)
                    if tt + 1 < NT:
                        te_stats(tt + 1)
                    sm.wait(T_("hid_rdy"), te, (gt + 1) * NHC)
                    for dc in range(NDC):
                        q = gt * NDC + dc
                        dgi = base + 8 + dc // 2
                        if dc % 2 == 0:
                            need_group(dgi)
                        if q >= 2:
                            sm.wait(T_("pd_free"), te, q - 1)
                        dbuf = wstrA if dgi % 2 == 0 else wstrB
                        for hc in range(NHC):
                            m = dc * NHC + hc
                            w = dbuf[:, (m % GT) * 128:(m % GT) * 128 + 128]
                            te.matmul(pD[q % 2][:], w, hidh[:, hc * 512:(hc + 1) * 512],
                                      start=(hc == 0), stop=False)
                            i = te.matmul(pD[q % 2][:], w, hidl[:, hc * 512:(hc + 1) * 512],
                                          start=False, stop=(hc == NHC - 1))
                        sm.post(T_("dnq"), i)

            if DBG_FFN and tag == "f1_":
                @blk.gpsimd
                def _(gp):
                    sm.wait(T_("xn_rdy"), gp, 1)
                    sm.post(T_("dbg1"), gp.dma_start(
                        out=d.dbgxn[:, 0:NDC * 512], in_=xnh[:]), 16)
                    sm.post(T_("dbg1"), gp.dma_start(
                        out=d.dbgxn[:, NDC * 512:], in_=xnl[:]), 16)
                    sm.wait(T_("hid_rdy"), gp, NHC)
                    sm.post(T_("dbg2"), gp.dma_start(
                        out=d.dbghid[:, 0:NHC * 512], in_=hidh[:]), 16)
                    sm.post(T_("dbg2"), gp.dma_start(
                        out=d.dbghid[:, NHC * 512:], in_=hidl[:]), 16)
                    sm.wait(T_("dbg1"), gp)
                    sm.wait(T_("dbg2"), gp)

            def ve_arith(ve, tt2):
                gt2 = T0 + tt2
                sm.wait(T_("sums_done"), ve, gt2 + 1)
                if DBG_FFN and tag == "f1_" and tt2 == 1:
                    sm.wait(T_("dbg1"), ve, 32)
                    sm.wait(T_("dbg2"), ve, 32)
                ve.tensor_scalar(meanB[:], pS1[:], 1.0 / C, None, ALU.mult)
                sm.post(T_("s1s2_ev"), ve.tensor_scalar(varB[:], pS2[:],
                                                        1.0 / C, None, ALU.mult))
                sm.wait(T_("m2_rdy"), ve, gt2 + 1)
                sm.post(T_("var_rdy"), ve.tensor_sub(varB[:], varB[:], rstdB[:]))
                sm.wait(T_("sd_rdy"), ve, gt2 + 1)
                sm.post(T_("stats_done"), ve.reciprocal(rstdB[:], varB[:]))

            def ve_xnsplit(ve, tt2):
                gt2 = T0 + tt2
                if gt2 > 0:
                    sm.wait(T_("uq"), ve, gt2 * NHC)
                for dc in range(NDC):
                    xsl = s.xres[:, dc * T + tt2 * 512: dc * T + tt2 * 512 + 512]
                    ve.tensor_sub(tmpf[:], xsl, meanB[:])
                    ve.tensor_mul(xnf[:], tmpf[:], rstdB[:])
                    ve.tensor_copy(xnh[:, dc * 512:(dc + 1) * 512], xnf[:])
                    i = ve.scalar_tensor_tensor(
                        xnl[:, dc * 512:(dc + 1) * 512], xnf[:], 1.0,
                        xnh[:, dc * 512:(dc + 1) * 512], ALU.mult, ALU.subtract)
                sm.post(T_("xn_rdy"), i)

            @blk.vector
            def _(ve):
                for tt in range(NT):
                    gt = T0 + tt
                    if tt == 0:
                        ve_arith(ve, tt)
                        ve_xnsplit(ve, tt)
                    if gt > 0:
                        sm.wait(T_("dnq"), ve, gt * NDC)
                    for hc in range(NHC):
                        g = gt * NHC + hc
                        sm.wait(T_("pg_free"), ve, g + 1)
                        sm.wait(T_("uq"), ve, g + 1)
                        i = ve.scalar_tensor_tensor(hf[:], gsil[:], S127,
                                                    pU[g % 2][:], ALU.mult, ALU.mult)
                        sm.post(T_("hf_rdy"), i)
                        sm.post(T_("pu_free"), ve.tensor_copy(tmpf[:, 0:1], hf[:, 0:1]))
                        sm.wait(T_("hh_rdy"), ve, g + 1)
                        sm.post(T_("hid_rdy"), ve.scalar_tensor_tensor(
                            hidl[:, hc * 512:(hc + 1) * 512], hf[:], 1.0,
                            hidh[:, hc * 512:(hc + 1) * 512], ALU.mult, ALU.subtract))
                    if tt + 1 < NT:
                        ve_arith(ve, tt + 1)
                        ve_xnsplit(ve, tt + 1)
                    for dc in range(NDC):
                        q = gt * NDC + dc
                        sm.wait(T_("dnq"), ve, q + 1)
                        xsl = s.xres[:, dc * T + tt * 512: dc * T + tt * 512 + 512]
                        sm.post(T_("pd_free"), ve.scalar_tensor_tensor(
                            xsl, pD[q % 2][:], S3, xsl, ALU.mult, ALU.add))
        esF.close()

    if "ffn1" in PHASES:
        emit_ffn("f1_", 0)

    def emit_attnb():
        esB = ExitStack()
        sbufB, psumB = make_alloc(esB)
        TW = 256
        NTT = T // TW
        P3LV = 3
        P3CH = 3
        NH = 16
        n3 = sbufB("b_n3", [128, 2048])
        xxB = sbufB("b_xxB", [128, 2048])
        mixP = [sbufB("b_mixA", [128, 2048]), sbufB("b_mixB", [128, 2048])]
        rB = sbufB("b_r", [128, 2048])
        kB = sbufB("b_k", [128, 2048])
        tmp = sbufB("b_tmp", [128, 2048])
        k2tm = sbufB("b_k2tm", [128, 2048])
        vtm = sbufB("b_vtm", [128, 2048])
        wtm = sbufB("b_wtm", [128, 2048])
        ytm = sbufB("b_ytm", [128, 1024])
        lamP = sbufB("b_lamP", [128, 1024])
        lamEP = sbufB("b_lamEP", [128, 1024])
        SB = sbufB("b_S", [128, 512])
        carry8 = sbufB("b_carry", [128, 8])
        meanB = sbufB("b_mean", [128, TW])
        rstdB = sbufB("b_rstd", [128, TW])
        varB = sbufB("b_var", [128, TW])
        loraA = sbufB("b_loraA", [128, TW])
        loraB2 = sbufB("b_loraB", [128, TW])
        loraC = sbufB("b_loraC", [128, TW])
        tanh65 = sbufB("b_tanh65", [128, TW])
        tanhg = sbufB("b_tanhg", [128, TW])
        emal = sbufB("b_emal", [128, 64])
        dtm = sbufB("b_dtm", [128, 32])
        gnA = sbufB("b_gnA", [128, 16])
        gnB = sbufB("b_gnB", [128, 16])
        gnC = sbufB("b_gnC", [128, 16])
        rm64 = sbufB("b_rm64", [128, 1])
        dbgc = sbufB("b_dbgc", [128, 8])
        sc1 = sbufB("b_sc1", [128, 1])
        sc2 = sbufB("b_sc2", [128, 1])
        w2aug = [sbufB("b_w2a", [128, 1024]), sbufB("b_w2b", [128, 1024]),
                 sbufB("b_w2c", [128, 1024])]
        mw1 = sbufB("b_mw1", [128, NDC * 160])
        dw1 = sbufB("b_dw1", [128, NDC * 64])
        dw2td = sbufB("b_dw2td", [128, 1024])
        gw1 = sbufB("b_gw1", [128, NDC * 64])
        gw2 = sbufB("b_gw2", [128, 1024])
        maapp = sbufB("b_maapp", [128, 48])
        lngb = sbufB("b_lngb", [128, 16])
        Ublk = sbufB("b_ublk", [128, 128])
        maskSU = sbufB("b_mask", [128, 512], F16)
        triI = sbufB("b_triI", [128, 128])
        triS = sbufB("b_triS", [128, 128])
        idn = sbufB("b_idn", [128, 128])
        wpl = sbufB("b_wpl", [128, 2 * 1024])
        pL1 = psumB("b_pL1", [128, 512])
        pL2 = psumB("b_pL2", [128, 512])
        pP0 = psumB("b_pP0", [128, 512])
        pP1 = psumB("b_pP1", [128, 512])
        pQ = psumB("b_pQ", [128, 1024])
        pW0 = psumB("b_pW0", [128, 512])
        pW1 = psumB("b_pW1", [128, 512])

        MIX_F = ["w", "k", "v", "r", "g"]
        PLANES = [d.Wk, d.Wv, d.Wr, d.Wo]
        PRJSLOT = [(pL1, 0), (pL1, 256), (pL2, 0), (pL2, 256),
                   (pQ, 0), (pQ, 256), (pQ, 512), (pQ, 768)]

        def xsl_(dc2, tt):
            return s.xres[:, dc2 * T + tt * TW: dc2 * T + tt * TW + TW]


        _tesig = [None]

        def sig_mm(te, sig):
            if _tesig[0] != sig:
                te.drain()
                _tesig[0] = sig

        def pairview(t_, pr):
            return bass.AP(t_, pr * 128, [[2048, 128], [256, 8], [1, 128]])

        def lamview(t_, off=0, base=0):
            return bass.AP(t_, base + off,
                           [[2048, 128], [128, 8], [64, 2], [1, 64]])

        def colX(t_, off, base=0):
            return bass.AP(t_, base + off,
                           [[2048, 128], [128, 8], [64, 2], [0, 64]])

        def eview(t_, base=0):
            return bass.AP(t_, base, [[2048, 128], [128, 8], [1, 128]])

        def SBp(h):
            return pW1[(h % 2) * 64:(h % 2) * 64 + 64,
                       (h // 2) * 64:(h // 2) * 64 + 64]

        def p3_scalar_tail(sc, tt):
            if P3LV < 1:
                return
            sm.wait("b_wtme", sc, 2 * tt + 2)
            sm.post("b_ew", sc.activation(wtm[:], wtm[:], AF.Exp))
            for pr in range(2):
                for j in range(3):
                    sm.wait("b_esub", sc, tt * 6 + pr * 3 + j + 1)
                    sm.post("b_eexp", sc.activation(
                        mixP[1][:, 0:1024], mixP[1][:, 0:1024], AF.Exp))
                sm.wait("b_tpe", sc, tt * 4 + pr * 2 + 2)
                for ch2 in range(2):
                    ch = pr * 2 + ch2
                    sc.activation(emal[:, ch * 8:ch * 8 + 8],
                                  bass.AP(mixP[0], ch2 * 64 + 31,
                                          [[2048, 128], [128, 8]]),
                                  AF.Exp, scale=-1.0)
                    i = sc.activation(emal[:, 32 + ch * 8:32 + ch * 8 + 8],
                                      bass.AP(mixP[0], ch2 * 64 + 63,
                                              [[2048, 128], [128, 8]]),
                                      AF.Exp, scale=-1.0)
                sm.post("b_emal", i)
            if P3LV < 3:
                return
            for pr in range(2):
                sm.wait("b_ysum", sc, tt * 4 + pr * 2 + 2)
                sm.post("b_ysq", sc.square(mixP[1][:, 1024:2048], ytm[:]))
                sm.wait("b_gnv", sc, tt * 2 + pr + 1)
                sm.post("b_gnsd", sc.activation(
                    gnB[:], gnB[:], AF.Sqrt, bias=s.cst[:, 1:2]))

        def p3_te_tail(te, tt):
            if P3LV < 1:
                return
            sm.wait("b_rk", te, tt + 1)
            if tt > 0:
                sm.wait("b_zslot", te, tt * 16)
            sig_mm(te, ("mm", 0, 128))
            for dc2 in range(NDC):
                i = te.matmul(pW1[0:16, 0:TW], Ublk[:, dc2 * 16:(dc2 + 1) * 16],
                              tmp[:, dc2 * TW:(dc2 + 1) * TW],
                              start=(dc2 == 0), stop=(dc2 == NDC - 1))
            sm.post("b_dq", i)
            sm.wait("b_dst", te, tt + 1)
            sm.wait("b_gl", te, 2 * tt + 2)
            if tt > 0:
                sm.wait("b_zslot", te, tt * 16)
            sig_mm(te, ("tp", 0, 16))
            te.transpose(pW0[:, 0:16], tmp[0:16, 0:128], idn[0:16, 0:16])
            sm.post("b_dtp", te.transpose(pW0[:, 16:32], tmp[0:16, 128:256],
                                          idn[0:16, 0:16]))
            for pr in range(2):
                P = tt * 2 + pr
                sm.wait("b_ew", te, tt + 1)
                if P == 0:
                    sm.wait("b_tri", te, 1)
                if P > 0:
                    sm.wait("b_k2e", te, 2 * P)
                if pr == 0 and tt > 0 and P3CH >= 2:
                    sm.wait("b_ysum", te, tt * 4)
                sig_mm(te, ("mm", 0, 128))
                te.matmul(pQ[:, 0:512], triI[:],
                          wtm[:, pr * 1024:pr * 1024 + 512],
                          start=True, stop=True)
                i = te.matmul(pQ[:, 512:1024], triI[:],
                              wtm[:, pr * 1024 + 512:(pr + 1) * 1024],
                              start=True, stop=True)
                sm.post("b_lq", i)
                sm.wait("b_lev", te, 4 * P + 1)
                sig_mm(te, ("mm", 0, 128))
                te.matmul(pQ[:, 0:512], triS[:],
                          wtm[:, pr * 1024:pr * 1024 + 512],
                          start=True, stop=True)
                i = te.matmul(pQ[:, 512:1024], triS[:],
                              wtm[:, pr * 1024 + 512:(pr + 1) * 1024],
                              start=True, stop=True)
                sm.post("b_lq", i)
                sm.wait("b_lev", te, 4 * P + 2)
                for grp in range(2):
                    gI = P * 2 + grp
                    if grp == 0:
                        sm.wait("b_dev", te, tt + 1)
                    if gI > 0:
                        sm.wait("b_tpe", te, gI)
                    sig_mm(te, ("tp", 0, 128))
                    for q in range(4):
                        pc = grp * 4 + q
                        te.transpose(pW0[:, q * 128:(q + 1) * 128],
                                     lamP[:, pc * 128:(pc + 1) * 128], idn[:])
                        i = te.transpose(pW1[:, q * 128:(q + 1) * 128],
                                         lamEP[:, pc * 128:(pc + 1) * 128],
                                         idn[:])
                    sm.post("b_tpq", i)
                sm.wait("b_k2f", te, P + 1)
                sm.wait("b_lev", te, 4 * P + 4)
                for grp in range(2):
                    sig_mm(te, ("tp", 0, 128))
                    for q in range(4):
                        pc = grp * 4 + q
                        i = te.transpose(pQ[:, q * 128:(q + 1) * 128] if grp == 0
                                         else pQ[:, 512 + q * 128:
                                                 512 + (q + 1) * 128],
                                         bass.AP(tmp, pr * 1024 + pc * 128,
                                                 [[2048, 128], [1, 128]]),
                                         idn[:])
                    sm.post("b_k2q", i)
            if P3LV < 2:
                return
            for pr in range(2):
                P = tt * 2 + pr
                for cc in range(2):
                    ch = pr * 2 + cc
                    sm.wait("b_fac", te, tt * 6 + pr * 3 + 3)
                    sm.wait("b_csb", te, tt * 4 + ch)
                    if P3CH >= 3:
                        sm.wait("b_supd", te, tt * 4 + ch)
                    if cc == 0 and P3LV >= 3:
                        sm.wait("b_yhev", te, P * 2)
                        sm.wait("b_zslot", te, P * 8)
                    if ch == 0:
                        sm.wait("b_tpe", te, (tt * 2 + 2) * 2)
                        sm.wait("b_dev", te, tt + 1)
                    for h in [0, 2, 4, 6, 8, 10, 12, 14, 1, 3, 5, 7, 9, 11, 13, 15]:
                        sig_mm(te, ("mm", (h % 2) * 64, 64))
                        kv = bass.AP(kB, (h // 2) * 256 + pr * 128 + cc * 64 +
                                     ((h % 2) * 64) * 2048, [[2048, 64], [1, 64]])
                        rv = bass.AP(rB, (h // 2) * 256 + pr * 128 + cc * 64 +
                                     ((h % 2) * 64) * 2048, [[2048, 64], [1, 64]])
                        if h < 8:
                            out = pW0[cc * 64:cc * 64 + 64, h * 64:h * 64 + 64]
                        else:
                            out = pW1[cc * 64:cc * 64 + 64,
                                      (h - 8) * 64:(h - 8) * 64 + 64]
                        i = te.matmul(out, kv, rv, start=True, stop=True,
                                      skip_group_check=True)
                    sm.post("b_gq", i)
                    if P3CH < 2:
                        continue
                    sm.wait("b_csb", te, tt * 4 + ch + 1)
                    if P3CH >= 2:
                        sm.wait("b_ysum", te, tt * 4 + ch)
                    sig_mm(te, ("mm", cc * 64, 64))
                    for h in range(NH):
                        cs = bass.AP(wtm, h * 64 + (cc * 64) * 2048,
                                     [[2048, 64], [1, 64]])
                        vv = bass.AP(vtm, pr * 1024 + h * 64 + (cc * 64) * 2048,
                                     [[2048, 64], [1, 64]])
                        i = te.matmul(pQ[cc * 64:cc * 64 + 64,
                                         h * 64:h * 64 + 64],
                                      cs, vv, start=True, stop=True,
                                      skip_group_check=True)
                    sm.post("b_yint", i)
                    sm.wait("b_ssclv", te, tt * 4 + ch + 1)
                    sm.wait("b_yie", te, tt * 4 + ch + 1)
                    for h in [0, 2, 4, 6, 8, 10, 12, 14, 1, 3, 5, 7, 9, 11, 13, 15]:
                        sig_mm(te, ("mm", (h % 2) * 64, 64))
                        ss = bass.AP(wtm, 1024 + (h // 2) * 64 +
                                     ((h % 2) * 64) * 2048, [[2048, 64], [1, 64]])
                        rv = bass.AP(rB, (h // 2) * 256 + pr * 128 + cc * 64 +
                                     ((h % 2) * 64) * 2048, [[2048, 64], [1, 64]])
                        i = te.matmul(pQ[cc * 64:cc * 64 + 64,
                                         h * 64:h * 64 + 64],
                                      rv, ss, start=True, stop=True,
                                      skip_group_check=True)
                    sm.post("b_yq", i)
                    if P3CH < 3:
                        continue
                    sig_mm(te, ("mm", cc * 64, 64))
                    for h in range(NH):
                        k2 = bass.AP(k2tm, pr * 1024 + h * 64 + (cc * 64) * 2048,
                                     [[2048, 64], [1, 64]])
                        vv = bass.AP(vtm, pr * 1024 + h * 64 + (cc * 64) * 2048,
                                     [[2048, 64], [1, 64]])
                        i = te.matmul(SBp(h), k2, vv, start=True, stop=True,
                                      skip_group_check=True)
                    sm.post("b_incq", i)
                if P3LV < 3:
                    continue
                sm.wait("b_gnap", te, P + 1)
                sm.wait("b_csb", te, tt * 4 + (pr + 1) * 2)
                for grp in range(2):
                    if P * 2 + grp > 0:
                        sm.wait("b_yhev", te, P * 2 + grp)
                    sig_mm(te, ("tp", 0, 128))
                    for q in range(4):
                        jc = grp * 4 + q
                        i = te.transpose(pW0[:, q * 128:(q + 1) * 128],
                                         ytm[:, jc * 128:(jc + 1) * 128], idn[:])
                    sm.post("b_yhq", i)
                sm.wait("b_supd", te, tt * 4 + (pr + 1) * 2)
                sm.wait("b_yhev", te, P * 2 + 2)
                sig_mm(te, ("mm", 0, 64))
                for jc in range(NDC):
                    if P * 8 + jc >= 2:
                        sm.wait("b_zslot", te, P * 8 + jc - 1)
                    gb = pW0 if jc % 2 == 0 else pW1
                    gcol = ((jc // 2) % 2) * 128
                    sm.post("b_zgq", te.matmul(
                        gb[:, gcol:gcol + 128],
                        gw2[0:64, jc * 128:(jc + 1) * 128],
                        tanhg[0:64, pr * 128:(pr + 1) * 128],
                        start=True, stop=True))
            if P3LV < 3:
                return
            sm.wait("b_zslot", te, tt * 16 + 16)
            sm.wait("b_ysum", te, tt * 4 + 4)
            if tt > 0:
                sm.wait("b_xev", te, tt * 8)
            sig_mm(te, ("mm", 0, 128))
            for dc2 in range(NDC):
                gpl = tt * 32 + 24 + dc2
                sm.wait("b_wpldE" if gpl % 2 == 0 else "b_wpldO", te,
                                16 * (gpl // 2 + 1))
                pw = wpl[:, (gpl % 2) * 1024:(gpl % 2 + 1) * 1024]
                for ec in range(NDC):
                    pt, pcol = PRJSLOT[ec]
                    i = te.matmul(pt[:, pcol:pcol + TW],
                                  pw[:, ec * 128:(ec + 1) * 128],
                                  bass.AP(n3, dc2 * 128,
                                          [[2048, 128], [1024, 2], [1, 128]]),
                                  start=(dc2 == 0 and ec % 2 == 0),
                                  stop=(dc2 == NDC - 1 and ec % 2 == 1))
                sm.post("b_wod", i)

        def p3_vec_tail(ve, tt):
            if P3LV < 1:
                return
            sm.post("b_rk", ve.tensor_mul(tmp[:], rB[:], kB[:]))
            sm.wait("b_dq", ve, tt + 1)
            sm.post("b_dst", ve.tensor_copy(tmp[0:16, 0:TW], pW1[0:16, 0:TW]))
            sm.wait("b_dtp", ve, tt + 1)
            ve.tensor_copy(dtm[:, 0:16], pW0[:, 0:16])
            sm.post("b_dev", ve.tensor_copy(dtm[:, 16:32], pW0[:, 16:32]))
            for pr in range(2):
                P = tt * 2 + pr
                sm.wait("b_lq", ve, 2 * P + 1)
                sm.post("b_lev", ve.tensor_copy(lamP[:], pQ[:]))
                sm.wait("b_lq", ve, 2 * P + 2)
                i = ve.tensor_copy(lamEP[:], pQ[:])
                sm.post("b_lev", i, 3)
                for grp in range(2):
                    sm.wait("b_tpq", ve, P * 2 + grp + 1)
                    ve.tensor_copy(mixP[0][:, grp * 512:(grp + 1) * 512], pW0[:])
                    sm.post("b_tpe", ve.tensor_copy(
                        mixP[0][:, 1024 + grp * 512:1024 + (grp + 1) * 512],
                        pW1[:]))
                sm.post("b_esub", ve.tensor_tensor(
                    lamview(mixP[1]), lamview(mixP[0]), colX(mixP[0], 63),
                    ALU.subtract))
                sm.wait("b_eexp", ve, tt * 6 + pr * 3 + 1)
                sm.post("b_k2f", ve.tensor_tensor(
                    eview(tmp, pr * 1024), pairview(kB, pr), eview(mixP[1]),
                    ALU.mult))
                sm.post("b_esub", ve.tensor_tensor(
                    lamview(mixP[1]), lamview(mixP[0]), colX(mixP[0], 31),
                    ALU.subtract))
                sm.wait("b_eexp", ve, tt * 6 + pr * 3 + 2)
                ve.tensor_tensor(pairview(kB, pr), pairview(kB, pr),
                                 eview(mixP[1]), ALU.mult)
                sm.post("b_esub", ve.tensor_tensor(
                    lamview(mixP[1]), colX(mixP[0], 31),
                    lamview(mixP[0], 0, 1024), ALU.subtract))
                sm.wait("b_eexp", ve, tt * 6 + pr * 3 + 3)
                sm.post("b_fac", ve.tensor_tensor(
                    pairview(rB, pr), pairview(rB, pr), eview(mixP[1]),
                    ALU.mult), 3)
                for grp in range(2):
                    sm.wait("b_k2q", ve, P * 2 + grp + 1)
                    sm.post("b_k2e", ve.tensor_copy(
                        k2tm[:, pr * 1024 + grp * 512:
                             pr * 1024 + (grp + 1) * 512],
                        pQ[:, 0:512] if grp == 0 else pQ[:, 512:1024]))
            if P3LV < 2:
                return
            for pr in range(2):
                P = tt * 2 + pr
                for cc in range(2):
                    ch = pr * 2 + cc
                    sm.wait("b_gq", ve, tt * 4 + ch + 1)
                    ve.tensor_tensor(
                        bass.AP(wtm, (cc * 64) * 2048, [[2048, 64], [1, 512]]),
                        pW0[cc * 64:cc * 64 + 64, :],
                        maskSU[cc * 64:cc * 64 + 64, :], ALU.mult)
                    sm.post("b_csb", ve.tensor_tensor(
                        bass.AP(wtm, 512 + (cc * 64) * 2048,
                                [[2048, 64], [1, 512]]),
                        pW1[cc * 64:cc * 64 + 64, :],
                        maskSU[cc * 64:cc * 64 + 64, :], ALU.mult))
                    sm.wait("b_emal", ve, tt * 2 + pr + 1)
                    if P3CH < 2:
                        continue
                    sm.post("b_ssclv", ve.tensor_tensor(
                        wtm[:, 1024:1536].rearrange("q (a b) -> q a b", a=8),
                        SB[:].rearrange("q (a b) -> q a b", a=8),
                        bass.AP(emal, ch * 8, [[64, 128], [1, 8], [0, 64]]),
                        ALU.mult))
                    if cc == 0:
                        if P3LV >= 3:
                            sm.wait("b_yhq", ve, (tt * 2 + pr) * 2)
                        sm.wait("b_dev", ve, tt + 1)
                        ve.tensor_tensor(
                            ytm[:].rearrange("q (a b) -> q a b", a=16),
                            vtm[:, pr * 1024:(pr + 1) * 1024].rearrange(
                                "q (a b) -> q a b", a=16),
                            bass.AP(dtm, pr * 16, [[32, 128], [1, 16], [0, 64]]),
                            ALU.mult)
                    sm.wait("b_yint", ve, tt * 4 + ch + 1)
                    if DBG_P3 and tt == 0 and ch == 0:
                        ve.tensor_copy(mixP[1][:, 1024:2048], pQ[:])
                    sm.post("b_yie", ve.tensor_add(
                        bass.AP(ytm, (cc * 64) * 1024, [[1024, 64], [1, 1024]]),
                        bass.AP(ytm, (cc * 64) * 1024, [[1024, 64], [1, 1024]]),
                        pQ[cc * 64:cc * 64 + 64, :]))
                    sm.wait("b_yq", ve, tt * 4 + ch + 1)
                    sm.post("b_ysum", ve.tensor_add(
                        bass.AP(ytm, (cc * 64) * 1024, [[1024, 64], [1, 1024]]),
                        bass.AP(ytm, (cc * 64) * 1024, [[1024, 64], [1, 1024]]),
                        pQ[cc * 64:cc * 64 + 64, :]))
                    if P3CH < 3:
                        continue
                    sm.wait("b_incq", ve, tt * 4 + ch + 1)
                    ve.tensor_tensor(SB[:].rearrange("q (a b) -> q a b", a=8),
                                     SB[:].rearrange("q (a b) -> q a b", a=8),
                                     bass.AP(emal, 32 + ch * 8,
                                             [[64, 128], [1, 8], [0, 64]]),
                                     ALU.mult)
                    sm.post("b_supd", ve.tensor_add(SB[:], SB[:], pW1[:]))
                if P3CH < 2:
                    continue
                if P3LV < 3:
                    continue
                sm.wait("b_ysq", ve, tt * 2 + pr + 1)
                ve.tensor_reduce(gnA[:],
                                 ytm[:].rearrange("q (a b) -> q a b", a=16),
                                 mybir.AxisListType.X, ALU.add)
                sm.post("b_gns", ve.tensor_reduce(
                    gnB[:],
                    mixP[1][:, 1024:2048].rearrange("q (a b) -> q a b", a=16),
                    mybir.AxisListType.X, ALU.add))
                sm.wait("b_gns", ve)
                ve.tensor_scalar(gnA[:], gnA[:], 1.0 / 64, None, ALU.mult)
                sm.post("b_gns", ve.tensor_scalar(
                    gnB[:], gnB[:], 1.0 / 64, None, ALU.mult))
                sm.wait("b_gns", ve)
                sm.post("b_gns", ve.tensor_tensor(gnC[:], gnA[:], gnA[:],
                                                  ALU.mult))
                sm.wait("b_gns", ve)
                sm.post("b_gnv", ve.tensor_sub(gnB[:], gnB[:], gnC[:]))
                sm.wait("b_gnsd", ve, tt * 2 + pr + 1)
                sm.post("b_gns", ve.reciprocal(gnC[:], gnB[:]))
                ve.tensor_tensor(ytm[:].rearrange("q (a b) -> q a b", a=16),
                                 ytm[:].rearrange("q (a b) -> q a b", a=16),
                                 bass.AP(gnA, 0, [[16, 128], [1, 16], [0, 64]]),
                                 ALU.subtract)
                sm.wait("b_gns", ve)
                sm.post("b_gnap", ve.tensor_tensor(
                    ytm[:].rearrange("q (a b) -> q a b", a=16),
                    ytm[:].rearrange("q (a b) -> q a b", a=16),
                    bass.AP(gnC, 0, [[16, 128], [1, 16], [0, 64]]),
                    ALU.mult))
                for grp in range(2):
                    sm.wait("b_yhq", ve, P * 2 + grp + 1)
                    sm.post("b_yhev", ve.tensor_copy(
                        n3[:, pr * 1024 + grp * 512:pr * 1024 + (grp + 1) * 512],
                        pW0[:]))
                for jc in range(NDC):
                    sm.wait("b_zgq", ve, P * 8 + jc + 1)
                    sm.post("b_gns", ve.tensor_scalar(
                        varB[:, 0:128],
                        n3[:, pr * 1024 + jc * 128:
                           pr * 1024 + (jc + 1) * 128],
                        lngb[:, jc:jc + 1], lngb[:, 8 + jc:9 + jc],
                        ALU.mult, ALU.add))
                    sm.wait("b_gns", ve)
                    gb = pW0 if jc % 2 == 0 else pW1
                    gcol = ((jc // 2) % 2) * 128
                    sm.post("b_zslot", ve.tensor_tensor(
                        n3[:, pr * 1024 + jc * 128:pr * 1024 + (jc + 1) * 128],
                        varB[:, 0:128], gb[:, gcol:gcol + 128], ALU.mult))
                    sm.wait("b_zslot", ve)
            if P3LV < 3:
                return
            sm.wait("b_wod", ve, tt * 8 + 8)
            for ec in range(NDC):
                pt, pcol = PRJSLOT[ec]
                xsl2 = s.xres[:, ec * T + tt * TW: ec * T + tt * TW + TW]
                sm.post("b_xev", ve.scalar_tensor_tensor(
                    xsl2, pt[:, pcol:pcol + TW], 1.0, xsl2,
                    ALU.mult, ALU.add))
            sm.post("b_zdone", ve.tensor_copy(varB[:, 0:1], meanB[:, 0:1]))
        with nc.Block("p3") as blk:
            @blk.gpsimd
            def _(gp):
                gp.memset(carry8[:], 0.0)
                gp.memset(SB[:], 0.0)
                for t_ in (loraA, loraB2):
                    gp.memset(t_[32:33, :], 1.0)
                    gp.memset(t_[96:97, :], 1.0)
                gp.memset(loraC[32:33, :], 1.0)
                gp.memset(tanh65[64:65, :], 1.0)
                gp.memset(Ublk[:], 0.0)
                gp.iota(tmp[:, 0:512], [[0, 8], [1, 64]], channel_multiplier=0,
                        allow_small_or_imprecise_dtypes=True)
                gp.iota(ytm[:, 0:128], [[1, 128]], channel_multiplier=0,
                        allow_small_or_imprecise_dtypes=True)
                gp.iota(rm64[0:64, :], [[0, 1]], channel_multiplier=1,
                        allow_small_or_imprecise_dtypes=True)
                gp.iota(rm64[64:128, :], [[0, 1]], channel_multiplier=1,
                        allow_small_or_imprecise_dtypes=True)
                gp.memset(sc1[0:64, :], 63.0)
                gp.memset(sc1[64:128, :], 127.0)
                for dc2 in range(NDC):
                    sm.post("b_swld", gp.dma_start(
                        out=mw1[:, dc2 * 160:(dc2 + 1) * 160],
                        in_=d.maa_w1[dc2 * 128:(dc2 + 1) * 128, :]), 16)
                    sm.post("b_swld", gp.dma_start(
                        out=dw1[:, dc2 * 64:(dc2 + 1) * 64],
                        in_=d.decay_w1[dc2 * 128:(dc2 + 1) * 128, :]), 16)
                    sm.post("b_swld", gp.dma_start(
                        out=gw1[:, dc2 * 64:(dc2 + 1) * 64],
                        in_=d.gate_w1[dc2 * 128:(dc2 + 1) * 128, :]), 16)
                sm.post("b_swld", gp.dma_start(out=dw2td[0:64, :],
                                               in_=d.decay_w2[:, :]), 16)
                sm.post("b_swld", gp.dma_start(
                    out=dw2td[64:65, :],
                    in_=bass.AP(d.time_decay, 0, [[C, 1], [1, C]])), 16)
                sm.post("b_swld", gp.dma_start(out=gw2[0:64, :],
                                               in_=d.gate_w2[:, :]), 16)
                for fi, (t_, base) in enumerate([(w2aug[0], 0), (w2aug[0], 64),
                                                 (w2aug[1], 0), (w2aug[1], 64),
                                                 (w2aug[2], 0)]):
                    sm.post("b_swld", gp.dma_start(out=t_[base:base + 32, :],
                                                   in_=d.maa_w2[fi, :, :]), 16)
                for fi, nm in enumerate(["maa_x", "maa_w", "maa_k",
                                         "maa_v", "maa_r", "maa_g"]):
                    sm.post("b_swld", gp.dma_start(
                        out=maapp[:, fi * 8:(fi + 1) * 8],
                        in_=bass.AP(getattr(d, nm), 0, [[1, 128], [128, NDC]]),
                        allow_slow_non_contiguous=True), 16)
                for fi, nm in enumerate(["ln_g", "ln_b"]):
                    sm.post("b_swld", gp.dma_start(
                        out=lngb[:, fi * 8:(fi + 1) * 8],
                        in_=bass.AP(getattr(d, nm), 0, [[1, 128], [128, NDC]]),
                        allow_slow_non_contiguous=True), 16)
                for fi2, (t_, base, nm) in enumerate([
                        (w2aug[0], 32, "maa_w"), (w2aug[0], 96, "maa_k"),
                        (w2aug[1], 32, "maa_v"), (w2aug[1], 96, "maa_r"),
                        (w2aug[2], 32, "maa_g")]):
                    sm.post("b_swld", gp.dma_start(
                        out=t_[base:base + 1, :],
                        in_=bass.AP(getattr(d, nm), 0, [[C, 1], [1, C]])), 16)
                for h in range(NH):
                    half = (h % 2) * 64
                    sm.post("b_swld", gp.dma_start(
                        out=Ublk[half:half + 64,
                                 (h // 2) * 16 + h:(h // 2) * 16 + h + 1],
                        in_=bass.AP(d.u, h * 64, [[1, 64], [64, 1]])), 16)
                sm.wait("b_swld", gp)

            @blk.vector
            def _(ve):
                sm.wait("b_swld", ve)
                ve.tensor_copy(dbgc[:, 0:1], rm64[:])
                sm.post("b_mask", ve.tensor_scalar(
                    maskSU[:], tmp[:, 0:512], rm64[:], None, ALU.is_gt))
                ve.tensor_scalar(idn[:], ytm[:, 0:128], s.rowv[:], None,
                                 ALU.is_equal)
                ve.tensor_scalar(triI[:], ytm[:, 0:128], s.rowv[:], None,
                                 ALU.is_ge)
                ve.tensor_scalar(tmp[:, 0:128], ytm[:, 0:128], sc1[:],
                                 None, ALU.is_le)
                ve.tensor_mul(triI[:], triI[:], tmp[:, 0:128])
                ve.tensor_scalar(triS[:], ytm[:, 0:128], s.rowv[:], None,
                                 ALU.is_gt)
                sm.post("b_tri", ve.tensor_mul(triS[:], triS[:], tmp[:, 0:128]))

            @blk.sync
            def _(sy):
                def plane_consumed(sy, p):
                    tp_, j = p // 32, p % 32
                    if j < 24:
                        sm.wait("b_prj", sy, tp_ * 24 + j + 1)
                    else:
                        sm.wait("b_wod", sy, tp_ * 8 + (j - 24) + 1)

                NPL = 32 if P3LV >= 3 else 24
                for g0 in range(NTT * NPL):
                    g = (g0 // NPL) * 32 + (g0 % NPL)
                    if g0 >= 2:
                        plane_consumed(sy, ((g0 - 2) // NPL) * 32 + ((g0 - 2) % NPL))
                    w, dc2 = PLANES[(g % 32) // 8], g % 8
                    sm.post("b_wpldE" if g % 2 == 0 else "b_wpldO",
                            sy.dma_start(
                                out=wpl[:, (g % 2) * 1024:(g % 2 + 1) * 1024],
                                in_=w[dc2 * 128:(dc2 + 1) * 128, :]), 16)
                sm.wait("b_wpldE", sy)
                sm.wait("b_wpldO", sy)

            @blk.scalar
            def _(sc):
                for tt in range(NTT):
                    if tt > 0:
                        sm.wait("b_zdone", sc, tt)
                    for dc2 in range(NDC):
                        i = sc.square(tmp[:, dc2 * TW:(dc2 + 1) * TW],
                                      xsl_(dc2, tt))
                    sm.post("b_sq", i)
                    sm.wait("b_s12", sc, tt + 1)
                    sm.post("b_m2", sc.square(rstdB[:], meanB[:]))
                    sm.wait("b_var", sc, tt + 1)
                    sm.post("b_sd", sc.activation(varB[:], varB[:], AF.Sqrt,
                                                  bias=s.cst[:, 0:1]))
                    sm.wait("b_lmm", sc, 2 * (tt + 1))
                    sc.activation(loraA[0:32, :], pP0[0:32, 0:TW], AF.Tanh)
                    sc.activation(loraA[64:96, :], pP0[32:64, 0:TW], AF.Tanh)
                    sc.activation(loraB2[0:32, :], pP0[64:96, 0:TW], AF.Tanh)
                    sc.activation(loraB2[64:96, :], pP0[96:128, 0:TW], AF.Tanh)
                    sm.post("b_tanh", sc.activation(loraC[0:32, :],
                                                    pP1[0:32, 0:TW], AF.Tanh))
                    sm.wait("b_wl", sc, 2 * tt + 1)
                    sm.post("b_wl", sc.activation(tanh65[0:64, :],
                                                  pW0[0:64, 0:TW], AF.Tanh))
                    sm.wait("b_gl", sc, 2 * tt + 1)
                    sm.post("b_gl", sc.activation(tanhg[0:64, :],
                                                  pW0[64:128, 0:TW], AF.Tanh))
                    p3_scalar_tail(sc, tt)

            def emit_proj(te, tt, fi):
                f = MIX_F[fi]
                sm.wait("b_mixv", te, tt * 40 + fi * 8 + 8)
                mx = mixP[fi % 2]
                if f == "w":
                    if tt > 0:
                        sm.wait("b_wl", te, 2 * tt)
                        sm.wait("b_gl", te, 2 * tt)
                    sig_mm(te, ("mm", 0, 128))
                    for dc2 in range(NDC):
                        i = te.matmul(pW0[0:64, 0:TW],
                                      dw1[:, dc2 * 64:(dc2 + 1) * 64],
                                      mx[:, dc2 * TW:(dc2 + 1) * TW],
                                      start=(dc2 == 0), stop=(dc2 == NDC - 1))
                    sm.post("b_wl", i)
                    sm.wait("b_wl", te, 2 * tt + 2)
                    sm.wait("b_pev", te, tt * 20)
                    if tt > 0:
                        sm.wait("b_xev", te, tt * 8)
                    sig_mm(te, ("mm", 0, 65))
                    for tc in range(2):
                        if tc == 1:
                            sm.wait("b_wtme", te, 2 * tt + 1)
                        te.matmul(pQ[:, 0:512],
                                  tanh65[0:65, tc * 128:(tc + 1) * 128],
                                  dw2td[0:65, 0:512], start=True, stop=True)
                        i = te.matmul(pQ[:, 512:1024],
                                      tanh65[0:65, tc * 128:(tc + 1) * 128],
                                      dw2td[0:65, 512:1024], start=True, stop=True)
                        sm.post("b_wtm", i)
                elif f == "g":
                    sig_mm(te, ("mm", 0, 128))
                    for dc2 in range(NDC):
                        i = te.matmul(pW0[64:128, 0:TW],
                                      gw1[:, dc2 * 64:(dc2 + 1) * 64],
                                      mx[:, dc2 * TW:(dc2 + 1) * TW],
                                      start=(dc2 == 0), stop=(dc2 == NDC - 1))
                    sm.post("b_gl", i)
                elif f == "v":
                    sm.wait("b_pev", te, tt * 20 + 8)
                    sig_mm(te, ("mm", 0, 128))
                    for dc2 in range(NDC):
                        gpl = tt * 32 + 8 + dc2
                        sm.wait("b_wpldE" if gpl % 2 == 0 else "b_wpldO", te,
                                16 * (gpl // 2 + 1))
                        pw = wpl[:, (gpl % 2) * 1024:(gpl % 2 + 1) * 1024]
                        st, sp = (dc2 == 0), (dc2 == NDC - 1)
                        te.matmul(pL1[:, 0:512],
                                  mx[:, dc2 * TW: dc2 * TW + 128],
                                  pw[:, 0:512], start=st, stop=sp)
                        te.matmul(pL2[:, 0:512],
                                  mx[:, dc2 * TW: dc2 * TW + 128],
                                  pw[:, 512:1024], start=st, stop=sp)
                        te.matmul(pQ[:, 0:512],
                                  mx[:, dc2 * TW + 128: dc2 * TW + 256],
                                  pw[:, 0:512], start=st, stop=sp)
                        i = te.matmul(pQ[:, 512:1024],
                                      mx[:, dc2 * TW + 128: dc2 * TW + 256],
                                      pw[:, 512:1024], start=st, stop=sp)
                        sm.post("b_prj", i)
                else:
                    base_pl = tt * 32 + (0 if f == "k" else 16)
                    if f == "k":
                        sm.wait("b_pev", te, tt * 20)
                        sm.wait("b_wtme", te, 2 * tt + 2)
                        if tt > 0:
                            sm.wait("b_xev", te, tt * 8)
                    else:
                        sm.wait("b_pev", te, tt * 20 + 12)
                    sig_mm(te, ("mm", 0, 128))
                    for dc2 in range(NDC):
                        gpl = base_pl + dc2
                        sm.wait("b_wpldE" if gpl % 2 == 0 else "b_wpldO", te,
                                16 * (gpl // 2 + 1))
                        pw = wpl[:, (gpl % 2) * 1024:(gpl % 2 + 1) * 1024]
                        for ec in range(NDC):
                            pt, pcol = PRJSLOT[ec]
                            i = te.matmul(pt[:, pcol:pcol + TW],
                                          pw[:, ec * 128:(ec + 1) * 128],
                                          mx[:, dc2 * TW:(dc2 + 1) * TW],
                                          start=(dc2 == 0 and ec % 2 == 0),
                                          stop=(dc2 == NDC - 1 and ec % 2 == 1))
                        sm.post("b_prj", i)

            @blk.tensor
            def _(te):
                for tt in range(NTT):
                    sm.wait("b_sq", te, tt + 1)
                    if tt > 0:
                        sm.wait("b_stats", te, tt)
                        sm.wait("b_pev", te, (tt - 1) * 20 + 16)
                        sm.wait("b_xev", te, tt * 8)
                    sig_mm(te, ("mm", 0, 128))
                    for dc2 in range(NDC):
                        te.matmul(pL1[:, 0:TW], s.ones128[:], xsl_(dc2, tt),
                                  start=(dc2 == 0), stop=(dc2 == NDC - 1))
                        i = te.matmul(pL2[:, 0:TW], s.ones128[:],
                                      tmp[:, dc2 * TW:(dc2 + 1) * TW],
                                      start=(dc2 == 0), stop=(dc2 == NDC - 1))
                    sm.post("b_sums", i)
                    sm.wait("b_xxx", te, tt + 1)
                    if tt > 0:
                        sm.wait("b_mixv", te, tt * 40)
                    sig_mm(te, ("mm", 0, 128))
                    for dc2 in range(NDC):
                        i = te.matmul(pP0[:, 0:TW],
                                      mw1[:, dc2 * 160: dc2 * 160 + 128],
                                      mixP[0][:, dc2 * TW:(dc2 + 1) * TW],
                                      start=(dc2 == 0), stop=(dc2 == NDC - 1))
                    sm.post("b_lmm", i)
                    for dc2 in range(NDC):
                        i = te.matmul(pP1[0:32, 0:TW],
                                      mw1[:, dc2 * 160 + 128: dc2 * 160 + 160],
                                      mixP[0][:, dc2 * TW:(dc2 + 1) * TW],
                                      start=(dc2 == 0), stop=(dc2 == NDC - 1))
                    sm.post("b_lmm", i)
                    for fi in range(5):
                        sm.wait("b_tanh", te, tt + 1)
                        lt = [loraA, loraA, loraB2, loraB2, loraC][fi]
                        lb = [0, 64, 0, 64, 0][fi]
                        wt = w2aug[fi // 2]
                        sig_mm(te, ("mm", lb, 33))
                        for ec in range(NDC):
                            slot = (pP0 if ec % 2 == 0 else pP1)
                            scol = 256 if fi % 2 == 1 else 0
                            gmm = tt * 40 + fi * 8 + ec
                            if gmm >= 2:
                                sm.wait("b_mixv", te, gmm - 1)
                            sm.post("b_mm", te.matmul(
                                slot[:, scol:scol + TW],
                                wt[lb:lb + 33, ec * 128:(ec + 1) * 128],
                                lt[lb:lb + 33, :], start=True, stop=True))
                        emit_proj(te, tt, fi)
                    p3_te_tail(te, tt)

            @blk.vector
            def _(ve):
                for tt in range(NTT):
                    sm.wait("b_sums", ve, tt + 1)
                    ve.tensor_scalar(meanB[:], pL1[:, 0:TW], 1.0 / C, None,
                                     ALU.mult)
                    sm.post("b_s12", ve.tensor_scalar(varB[:], pL2[:, 0:TW],
                                                      1.0 / C, None, ALU.mult))
                    sm.wait("b_m2", ve, tt + 1)
                    sm.post("b_var", ve.tensor_sub(varB[:], varB[:], rstdB[:]))
                    sm.wait("b_sd", ve, tt + 1)
                    sm.post("b_stats", ve.reciprocal(rstdB[:], varB[:]))
                    if tt > 0:
                        sm.wait("b_zdone", ve, tt)
                    for dc2 in range(NDC):
                        ve.tensor_sub(n3[:, dc2 * TW:(dc2 + 1) * TW],
                                      xsl_(dc2, tt), meanB[:])
                        ve.tensor_mul(n3[:, dc2 * TW:(dc2 + 1) * TW],
                                      n3[:, dc2 * TW:(dc2 + 1) * TW], rstdB[:])
                    ve.tensor_tensor(
                        bass.AP(xxB, 1, [[2048, 128], [TW, NDC], [1, TW - 1]]),
                        bass.AP(n3, 0, [[2048, 128], [TW, NDC], [1, TW - 1]]),
                        bass.AP(n3, 1, [[2048, 128], [TW, NDC], [1, TW - 1]]),
                        ALU.subtract)
                    i = ve.tensor_tensor(
                        bass.AP(xxB, 0, [[2048, 128], [TW, NDC]]),
                        carry8[:],
                        bass.AP(n3, 0, [[2048, 128], [TW, NDC]]),
                        ALU.subtract)
                    sm.post("b_xx", i)
                    sm.post("b_carry", ve.tensor_copy(
                        carry8[:], bass.AP(n3, TW - 1, [[2048, 128], [TW, NDC]])))
                    for dc2 in range(NDC):
                        i = ve.scalar_tensor_tensor(
                            mixP[0][:, dc2 * TW:(dc2 + 1) * TW],
                            xxB[:, dc2 * TW:(dc2 + 1) * TW],
                            maapp[:, dc2:dc2 + 1],
                            n3[:, dc2 * TW:(dc2 + 1) * TW], ALU.mult, ALU.add)
                    sm.post("b_xxx", i)
                    for fi in range(5):
                        f = MIX_F[fi]
                        mx = mixP[fi % 2]
                        for ec in range(NDC):
                            slot = (pP0 if ec % 2 == 0 else pP1)
                            scol = 256 if fi % 2 == 1 else 0
                            gmm = tt * 40 + fi * 8 + ec
                            sm.wait("b_mm", ve, gmm + 1)
                            ve.tensor_tensor(varB[:],
                                             xxB[:, ec * TW:(ec + 1) * TW],
                                             slot[:, scol:scol + TW], ALU.mult)
                            i = ve.tensor_add(mx[:, ec * TW:(ec + 1) * TW],
                                              varB[:],
                                              n3[:, ec * TW:(ec + 1) * TW])
                            sm.post("b_mixv", i)
                        if f == "w":
                            for tc in range(2):
                                sm.wait("b_wtm", ve, 2 * tt + tc + 1)
                                sm.post("b_wtme", ve.tensor_copy(
                                    wtm[:, tc * 1024:(tc + 1) * 1024], pQ[:]))
                        elif f in ("k", "r"):
                            dst = kB if f == "k" else rB
                            sm.wait("b_prj", ve,
                                    tt * 24 + (8 if f == "k" else 24))
                            for ec in range(NDC):
                                pt, pcol = PRJSLOT[ec]
                                sm.post("b_pev", ve.tensor_copy(
                                    dst[:, ec * TW:(ec + 1) * TW],
                                    pt[:, pcol:pcol + TW]))
                        elif f == "v":
                            sm.wait("b_prj", ve, tt * 24 + 16)
                            sm.post("b_pev", ve.tensor_copy(
                                vtm[:, 0:512], pL1[:, 0:512]))
                            sm.post("b_pev", ve.tensor_copy(
                                vtm[:, 512:1024], pL2[:, 0:512]))
                            sm.post("b_pev", ve.tensor_copy(
                                vtm[:, 1024:1536], pQ[:, 0:512]))
                            sm.post("b_pev", ve.tensor_copy(
                                vtm[:, 1536:2048], pQ[:, 512:1024]))
                    p3_vec_tail(ve, tt)

            if DBG_P3:
                @blk.gpsimd
                def _(gp):
                    sm.wait("b_zdone", gp, NTT)
                    off = 0
                    for t_ in [n3, rB, kB, k2tm, vtm, wtm, tmp]:
                        sm.post("b_dbg", gp.dma_start(
                            out=d.dbg3[:, off:off + 2048], in_=t_[:]), 16)
                        off += 2048
                    for t_, w_ in [(SB, 512), (emal, 64), (dtm, 32),
                                   (ytm, 1024), (lamP, 1024), (lamEP, 1024),
                                   (mixP[0], 2048), (triI, 128), (triS, 128),
                                   (idn, 128), (rm64, 1), (dbgc, 8),
                                   (mixP[1], 2048), (gnA, 16), (gnB, 16),
                                   (gnC, 16)]:
                        sm.post("b_dbg", gp.dma_start(
                            out=d.dbg3[:, off:off + w_], in_=t_[:],
                            allow_slow_non_contiguous=True), 16)
                        off += w_
                    sm.wait("b_dbg", gp)

        esB.close()

    if "attn_b" in PHASES:
        emit_attnb()

    if "ffn2" in PHASES:
        emit_ffn("f2_", 4)

    with nc.Block("pout") as blk:
        @blk.gpsimd
        def _(gp):
            sm.wait("outst", gp)

    return nc


_NC_CACHE = None


def _get_nc():
    global _NC_CACHE
    if _NC_CACHE is None:
        _NC_CACHE = build_nc()
    return _NC_CACHE


def kernel(**inputs):
    nc = _get_nc()
    xf = np.ascontiguousarray(np.asarray(inputs["L0_attn_a_X"], dtype=np.float32))
    maps = []
    for b in range(8):
        m = {
            "x0": np.ascontiguousarray(xf[b].T),
            "wva": np.asarray(inputs["L0_attn_a_Wv"], np.float32),
            "bva": np.asarray(inputs["L0_attn_a_bv"], np.float32),
            "wgi": np.asarray(inputs["L0_ffn_w_gate"], np.int32),
            "wui": np.asarray(inputs["L0_ffn_w_up"], np.int32),
            "wdi": np.asarray(inputs["L0_ffn_w_down"], np.int32),
        }
        for n in ["maa_x", "maa_w", "maa_k", "maa_v", "maa_r", "maa_g",
                  "maa_w1", "maa_w2", "Wr", "Wk", "Wv", "Wo", "gate_w1",
                  "gate_w2", "time_decay", "decay_w1", "decay_w2", "u",
                  "ln_g", "ln_b"]:
            m[n] = np.asarray(inputs["L0_attn_b_" + n], np.float32)
        maps.append(m)
    res = run_bass_kernel_spmd(nc, maps, core_ids=list(range(8)))
    out = np.stack([np.ascontiguousarray(r["out"].T) for r in res.results])
    return out


